# version 05
# baseline (speedup 1.0000x reference)
import contextlib
import numpy as np
import ml_dtypes
import concourse.bass as bass
import concourse.mybir as mybir
from concourse.bass_utils import run_bass_kernel_spmd

F32 = mybir.dt.float32
BF16 = mybir.dt.bfloat16
U8 = mybir.dt.uint8
I32 = mybir.dt.int32
AF = mybir.ActivationFunctionType
ALU = mybir.AluOpType
AX = mybir.AxisListType
BF = ml_dtypes.bfloat16

T = 2048
C = 1024
NCC = 8
H = 16
DH = 64
M_LM = 256
NS_ITERS = 6
HID = 4096
NHC = 32
EPS = 1e-5
N_CORES = 8
QMAGIC = 0x5F3759DF

ENGINES = ("sync", "scalar", "tensor", "vector", "gpsimd")
FFN_STOP = None
ATB_QUARTERS = 4
ATB_BARRIER = False
ATA_LOCAL = False
ATA_STOP = None
ATA_P2 = "full"
ATA_NS = NS_ITERS
ATA_NSOP = 99
ATA_P1 = NCC
RACE_DETECT = False
DUMPS = ()


def _dt(x):
    return x.ap().dtype if hasattr(x, "ap") else x.dtype


class Tape:
    def __init__(self):
        self.ops = []

    ambient = ()

    def op(self, engine, fn, deps=(), amount=1, force=(), stream=None):
        deps = tuple(d for d in deps if d is not None) + tuple(self.ambient)
        force = tuple(d for d in force if d is not None)
        self.ops.append(dict(engine=engine, fn=fn, deps=deps, force=force,
                             amount=amount, needed=False, cum=None, stream=stream))
        return len(self.ops) - 1

    def dma(self, engine, fn, deps=(), stream=None):
        if engine == "sync":
            assert stream is not None, "sync DMAs must declare a stream"
        return self.op(engine, fn, deps, amount=16, stream=stream)

    def last_per_engine(self):
        out = {}
        for i, o in enumerate(self.ops):
            out[o["engine"]] = i
        return tuple(out.values())

    SELF_SYNC = ("vector", "scalar", "gpsimd")

    def emit(self, nc, block, es):
        ops = self.ops
        for o in ops:
            if o["amount"] != 1:
                o["needed"] = True
            for d in o["deps"]:
                if (ops[d]["engine"] != o["engine"] or ops[d]["amount"] != 1
                        or o["engine"] in self.SELF_SYNC):
                    ops[d]["needed"] = True
            for d in o["force"]:
                ops[d]["needed"] = True
        def ekey(o):
            if o["amount"] == 16:
                if o["engine"] == "gpsimd":
                    return "gpsimd_dma"
                return "dma_" + o["stream"]
            return o["engine"]
        keys = list(ENGINES) + ["gpsimd_dma"]
        for o in ops:
            if ekey(o) not in keys:
                keys.append(ekey(o))
        sems = {e: es.enter_context(nc.semaphore(f"sem_{e}")) for e in keys}
        counters = {e: 0 for e in keys}
        for o in ops:
            if o["needed"]:
                counters[ekey(o)] += o["amount"]
                o["cum"] = counters[ekey(o)]

        def section(eng_name):
            def _(eng):
                waited = {}
                for i, o in enumerate(ops):
                    if o["engine"] != eng_name:
                        continue
                    need = {}
                    for d in o["deps"] + o["force"]:
                        od = ops[d]
                        if (d in o["deps"] and od["engine"] == eng_name
                                and od["amount"] == 1 and eng_name not in Tape.SELF_SYNC):
                            continue
                        v = od["cum"]
                        assert v is not None
                        if need.get(ekey(od), -1) < v:
                            need[ekey(od)] = v
                    for en2, v in need.items():
                        if waited.get(en2, -1) < v:
                            eng.wait_ge(sems[en2], v)
                            waited[en2] = v
                    inst = o["fn"](eng)
                    if o["needed"]:
                        assert inst is not None, f"needed op {i} on {eng_name} returned None"
                        inst.then_inc(sems[ekey(o)], o["amount"])
                return None
            return _

        for name in ENGINES:
            getattr(block, name)(section(name))


class K:
    def __init__(self, nc, es):
        self.nc = nc
        self.es = es
        self.tp = Tape()
        self.uid = 0
        self.pb = [es.enter_context(nc.psum_tensor(f"pb{i}", [128, 512], F32))
                   for i in range(8)]
        self.pb_readers = [[] for _ in range(8)]
        self.pb_group = [None] * 8
        self.pb_open = [False] * 8

    def name(self, base):
        self.uid += 1
        return f"{base}_{self.uid}"

    def sb(self, base, shape, dtype):
        return self.es.enter_context(self.nc.sbuf_tensor(self.name(base), shape, dtype))

    def mm1(self, bank, lhsT, rhs, start, stop, deps=(), out_ap=None):
        war = tuple(self.pb_readers[bank]) if start else ()
        out = out_ap if out_ap is not None else self.pb[bank][:, 0:rhs.shape[-1]]

        def fn(eng, out=out, lhsT=lhsT, rhs=rhs, start=start, stop=stop):
            return eng.matmul(out, lhsT, rhs, start=start, stop=stop)
        opid = self.tp.op("tensor", fn, deps=war + tuple(d for d in deps if d is not None))
        if start:
            self.pb_readers[bank] = []
        self.pb_group[bank] = opid
        return opid

    def mm(self, bank, calls, deps=(), ncols=512, out_ap=None):
        war = tuple(self.pb_readers[bank])
        out = out_ap if out_ap is not None else self.pb[bank][:, 0:ncols]
        n = len(calls)

        def fn(eng, calls=calls, out=out, n=n):
            inst = None
            for idx, (l, r) in enumerate(calls):
                inst = eng.matmul(out, l, r, start=(idx == 0), stop=(idx == n - 1))
            return inst
        opid = self.tp.op("tensor", fn, deps=war + tuple(d for d in deps if d is not None))
        self.pb_group[bank] = opid
        self.pb_readers[bank] = []
        return opid

    def rd(self, bank, engine, fn, deps=()):
        opid = self.tp.op(engine, fn,
                          deps=(self.pb_group[bank],) + tuple(d for d in deps if d is not None))
        self.pb_readers[bank].append(opid)
        return opid

    def vseq(self, engine, fns, deps=()):
        prev = None
        ids = []
        for i, fn in enumerate(fns):
            o = self.tp.op(engine, fn, deps=(deps if i == 0 else ()),
                           force=((prev,) if prev is not None else ()))
            ids.append(o)
            prev = o
        return ids

    def transpose_into(self, bank, region_off, in_ap, ident, deps=(), first=True):
        war = tuple(self.pb_readers[bank]) if first else ()
        in_part = in_ap.shape[0]
        in_free = in_ap.shape[-1]
        ps = self.pb[bank]
        if in_ap.dtype == BF16:
            out = ps.ap().bitcast(BF16)[0:in_free, region_off:region_off + in_part]
        else:
            out = ps[0:in_free, region_off:region_off + in_part]

        def fn(eng, out=out, in_ap=in_ap, ident=ident):
            return eng.transpose(out, in_ap, ident)
        opid = self.tp.op("tensor", fn, deps=war + tuple(d for d in deps if d is not None))
        if first:
            self.pb_readers[bank] = []
        self.pb_group[bank] = opid
        return opid



def make_const_arrays():
    cc = {}
    cc["ident_bf"] = np.eye(128, dtype=BF)
    cc["ident_f32"] = np.eye(128, dtype=np.float32)
    cc["ones_row_bf"] = np.ones((1, 512), dtype=BF)
    cc["ones_row_f32"] = np.ones((1, 128), dtype=np.float32)
    cc["ones_col_bf"] = np.ones((128, 1), dtype=BF)
    s2 = np.zeros((128, 2), dtype=BF)
    s2[:, 0] = 1.0
    cc["selA_bf"] = s2
    s2b = np.zeros((128, 2), dtype=BF)
    s2b[:, 1] = 1.0
    cc["selB_bf"] = s2b
    o2 = np.zeros((2, 256), dtype=BF)
    o2[0, 0:128] = 1.0
    o2[1, 128:256] = 1.0
    cc["rowsel_bf"] = o2
    i128 = np.eye(128, dtype=np.float32)
    z = np.zeros((128, 128), dtype=np.float32)
    lay = np.concatenate([i128, z, z, i128], axis=1)
    for kv in (7.0, 15.0, 13.0):
        cc[f"i{int(kv)}_bf"] = (lay * kv).astype(BF)
    sel = np.zeros((4, 512), dtype=np.float32)
    for e in range(4):
        sel[e, e * 128:(e + 1) * 128] = 1.0
    cc["sel4_bf"] = sel.astype(BF)
    idx = np.arange(128)
    cc["utri_bf"] = (idx[None, :] >= idx[:, None]).astype(BF)
    half = DH // 2
    inv_freq = (1.0 / 10000.0 ** (np.arange(0, DH, 2, dtype=np.float64) / DH))
    t = np.arange(T, dtype=np.float64)
    fr = np.outer(t, inv_freq)
    cosd = np.cos(fr)
    sind = np.sin(fr)
    Cm = np.zeros((128, T), dtype=np.float64)
    Sm = np.zeros((128, T), dtype=np.float64)
    for d in range(128):
        i = (d % DH) // 2
        Cm[d] = cosd[:, i]
        Sm[d] = sind[:, i] * (-1.0 if d % 2 == 0 else 1.0)
    cc["ropeC_bf"] = Cm.astype(BF)
    cc["ropeS_bf"] = Sm.astype(BF)
    P = np.zeros((128, 128), dtype=np.float32)
    for d in range(128):
        P[d, d ^ 1] = 1.0
    cc["perm_bf"] = P.astype(BF)
    hs = np.zeros((128, 2), dtype=np.float32)
    hs[:64, 0] = 1.0
    hs[64:, 1] = 1.0
    cc["headsel_bf"] = hs.astype(BF)
    cc["headselT_f32"] = np.ascontiguousarray(hs.T.astype(np.float32))
    return cc


DEFER_CONSTS = ("ropeC_bf", "ropeS_bf", "i7_bf", "i15_bf", "i13_bf")


def load_consts(k, carrs):
    out = {}
    deps = {}
    k.cdram = {}
    for name, arr in carrs.items():
        if name in DEFER_CONSTS:
            k.cdram[name] = k.nc.inline_tensor(np.ascontiguousarray(arr), k.name(f"c_{name}"))
            continue
        dt = BF16 if arr.dtype == BF else F32
        dram = k.nc.inline_tensor(np.ascontiguousarray(arr), k.name(f"c_{name}"))
        t = k.sb(f"c_{name}", list(arr.shape), dt)
        d = k.tp.dma("gpsimd", lambda e, t=t, dram=dram: e.dma_start(out=t[:], in_=dram.ap()))
        out[name] = t
        deps[name] = d
    return out, deps



def emit_ln(k, consts, cdeps, scr, src, src_deps, out, gamma=None, beta=None,
            banks=(6, 7), tag="ln", out_war=()):
    tp = k.tp
    s = scr
    cast_b = s["ln_cast"]
    sq_b = s["ln_sq"]
    mini = s["ln_mini"]
    stat, istat = s["ln_stat"], s["ln_istat"]
    ab2 = s["ln_ab2"]
    ast, bst = s["ln_ast"], s["ln_bst"]
    prev_users = s.get("ln_prev_users", ())
    gate_row = s.get("ln_gate_row", ())
    cast_prev = dict(s.get("ln_cast_prev", {}))
    mini_war = s.get("ln_mini_war", ())
    src_is_bf = _dt(src[0]) == BF16

    bounce = []
    for nt in range(4):
        sl = slice(nt * 512, (nt + 1) * 512)
        bank = banks[nt % 2]
        nmm = 0
        for ccx in range(NCC):
            if src_is_bf:
                rhs_ap = src[ccx][:, sl]
                cst = src_deps[ccx]
            else:
                b = ccx % 2
                cst = tp.op("gpsimd", lambda e, ccx=ccx, b=b, sl=sl: e.tensor_copy(
                    cast_b[b][:], src[ccx][:, sl]),
                    deps=(src_deps[ccx], cast_prev.get(("c", b))) + tuple(prev_users))
                rhs_ap = cast_b[b][:]
            m1 = k.mm1(bank, consts["selA_bf"][:], rhs_ap, start=(nmm == 0), stop=False,
                       deps=(cst, cdeps["selA_bf"]), out_ap=k.pb[bank][0:2, 0:512])
            if not src_is_bf:
                cast_prev[("c", ccx % 2)] = m1
            nmm += 1
            b2 = 0
            cst2 = tp.op("scalar", lambda e, ccx=ccx, b2=b2, sl=sl: e.activation(
                out=sq_b[b2][:], in_=src[ccx][:, sl], func=AF.Square),
                deps=(src_deps[ccx], cast_prev.get(("s", b2))) + tuple(prev_users))
            m2 = k.mm1(bank, consts["selB_bf"][:], sq_b[b2][:],
                       start=False, stop=(ccx == NCC - 1),
                       deps=(cst2, cdeps["selB_bf"]), out_ap=k.pb[bank][0:2, 0:512])
            cast_prev[("s", b2)] = m2
            nmm += 1
        evd = k.rd(bank, "scalar", lambda e, bank=bank: e.copy(
            out=mini[0:2, :], in_=k.pb[bank][0:2, 0:512]), deps=tuple(bounce) + tuple(mini_war))
        b1 = tp.dma("gpsimd", lambda e, nt=nt: e.dma_start(
            out=stat[:, nt * 4:(nt + 1) * 4], in_=mini[0:1, :]), deps=(evd,))
        b2d = tp.dma("gpsimd", lambda e, nt=nt: e.dma_start(
            out=stat[:, 16 + nt * 4:16 + (nt + 1) * 4], in_=mini[1:2, :]), deps=(evd,))
        bounce.extend([b1, b2d])

    S = stat
    m = S[:, 32:48]
    var = S[:, 48:64]
    y = S[:, 16:32]
    steps = [
        lambda e: e.tensor_scalar_mul(m, S[:, 0:16], 1.0 / C),
        lambda e: e.tensor_scalar_mul(var, S[:, 16:32], 1.0 / C),
        lambda e: e.tensor_mul(S[:, 0:16], m, m),
        lambda e: e.tensor_sub(var, var, S[:, 0:16]),
        lambda e: e.tensor_scalar_add(var, var, EPS),
        lambda e: e.tensor_single_scalar(istat[:, 0:16], var.bitcast(I32), 1, ALU.arith_shift_right),
        lambda e: e.tensor_scalar(istat[:, 0:16], istat[:, 0:16], -1, QMAGIC, ALU.mult, ALU.add),
        lambda e: e.tensor_copy(y, istat[:, 0:16].bitcast(F32)),
    ]
    for _ in range(3):
        steps += [
            lambda e: e.tensor_mul(S[:, 0:16], y, y),
            lambda e: e.tensor_mul(S[:, 0:16], S[:, 0:16], var),
            lambda e: e.tensor_scalar(S[:, 0:16], S[:, 0:16], -0.5, 1.5, ALU.mult, ALU.add),
            lambda e: e.tensor_mul(y, y, S[:, 0:16]),
        ]
    steps += [
        lambda e: e.tensor_copy(ast[:], y),
        lambda e: e.tensor_mul(S[:, 0:16], m, y),
        lambda e: e.tensor_scalar_mul(bst[:], S[:, 0:16], -1.0),
    ]
    rm = k.vseq("vector", steps, deps=tuple(bounce))[-1]
    rowd = []
    for nt in range(4):
        rowd.append(tp.dma("gpsimd", lambda e, nt=nt: e.dma_start(
            out=ab2[0:1, nt * 512:(nt + 1) * 512], in_=ast[:, nt * 4:(nt + 1) * 4]),
            deps=(rm,) + tuple(gate_row)))
        rowd.append(tp.dma("gpsimd", lambda e, nt=nt: e.dma_start(
            out=ab2[1:2, nt * 512:(nt + 1) * 512], in_=bst[:, nt * 4:(nt + 1) * 4]),
            deps=(rm,) + tuple(gate_row)))

    out_ops = [None] * NCC
    bca_mm = []
    apply_last = {}
    for nt in range(4):
        sl = slice(nt * 512, (nt + 1) * 512)
        gA = k.mm(banks[0], [(consts["rowsel_bf"][:, 0:128], ab2[0:2, sl])],
                  deps=tuple(rowd) + (cdeps["rowsel_bf"],))
        gB = k.mm(banks[1], [(consts["rowsel_bf"][:, 128:256], ab2[0:2, sl])],
                  deps=tuple(rowd))
        bca_mm.extend([gA, gB])
        for ccx in range(NCC):
            o1 = k.rd(banks[0], "vector", lambda e, ccx=ccx, sl=sl: e.tensor_mul(
                out[ccx][:, sl], src[ccx][:, sl], k.pb[banks[0]][:, 0:512]),
                deps=(src_deps[ccx],) + tuple(out_war))
            o2 = k.rd(banks[1], "vector", lambda e, ccx=ccx, sl=sl: e.tensor_add(
                out[ccx][:, sl], out[ccx][:, sl], k.pb[banks[1]][:, 0:512]), deps=())
            k.tp.ops[o2]["force"] = k.tp.ops[o2]["force"] + (o1,)
            apply_last[ccx] = o2
    users = []
    for ccx in range(NCC):
        a1 = apply_last[ccx]
        if gamma is not None:
            a1 = tp.op("vector", lambda e, ccx=ccx: e.tensor_scalar(
                out[ccx][:], out[ccx][:], gamma[ccx], beta[ccx], ALU.mult, ALU.add),
                force=(apply_last[ccx],))
        out_ops[ccx] = a1
        users.append(a1)
    s["ln_prev_users"] = tuple(users) + (rm,)
    s["ln_cast_prev"] = cast_prev
    s["ln_gate_row"] = tuple(bca_mm)
    s["ln_mini_war"] = (bounce[-2], bounce[-1])
    return out_ops


def alloc_ln_scratch(k):
    s = {}
    s["ln_cast"] = [k.sb("lncast", [128, 512], BF16) for _ in range(2)]
    s["ln_sq"] = [k.sb("lnsq", [128, 512], BF16) for _ in range(1)]
    s["ln_mini"] = k.sb("lnmini", [2, 512], F32)
    s["ln_stat"] = k.sb("lnstat", [128, 64], F32)
    s["ln_istat"] = k.sb("lnistat", [128, 16], I32)
    s["ln_ab2"] = k.sb("lnab2", [2, T], BF16)
    s["ln_ast"] = k.sb("lnast", [128, 16], BF16)
    s["ln_bst"] = k.sb("lnbst", [128, 16], BF16)
    return s



class Slab:

    def __init__(self, k, tag):
        self.k = k
        self.stream = tag
        self.raw = [k.sb(f"{tag}_raw", [128, 1024], F32)]
        self.bf = [k.sb(f"{tag}_bf", [128, 1024], BF16) for _ in range(2)]
        self.prev_cast = [None]
        self.readers = [[], []]
        self.i = 0

    def load(self, src_ap_fn, deps=()):
        k = self.k
        b = self.i % 2
        self.i += 1
        ld = k.tp.dma("sync", lambda e: e.dma_start(out=self.raw[0][:], in_=src_ap_fn()),
                      deps=(self.prev_cast[0],) + tuple(deps), stream=self.stream)
        cst = k.tp.op("gpsimd", lambda e, b=b: e.tensor_copy(self.bf[b][:], self.raw[0][:]),
                      deps=(ld,) + tuple(self.readers[b]))
        self.prev_cast[0] = cst
        self.readers[b] = []
        return self.bf[b], cst, b

    def used(self, b, opid):
        self.readers[b].append(opid)


def cc_slab_ap(wdram, mi):
    return wdram.rearrange("(cc p) o -> p cc o", p=128)[:, :, mi * 128:(mi + 1) * 128]


def emit_ccpass(k, slab, wdram, rhs, rhs_deps, consume, extra_mm_deps=(), tag="cc"):
    groups = []
    rot = getattr(k, "_rot", 0)
    for mi in range(NCC):
        wbf, cst, b = slab.load(lambda mi=mi: cc_slab_ap(wdram, mi))
        for nt in range(4):
            sl = slice(nt * 512, (nt + 1) * 512)
            bank = rot % 6
            rot += 1
            calls = [(wbf[:, cc * 128:(cc + 1) * 128], rhs[cc][:, sl]) for cc in range(NCC)]
            g = k.mm(bank, calls, deps=(cst,) + tuple(rhs_deps) + tuple(extra_mm_deps))
            slab.used(b, g)
            groups.append(g)
            consume(mi, nt, bank)
    k._rot = rot
    return groups



def emit_ffn(k, consts, cdeps, scr, W, sm, X, x_deps, bufs, tag="ffn"):
    tp = k.tp
    ARENA, ACT1, ACT2, GB, GS = bufs["ARENA"], bufs["ACT1"], bufs["ACT2"], bufs["GB"], bufs["GS"]
    GROWS, SMW, RL, SGT = bufs["GROWS"], bufs["SMW"], bufs["RL"], bufs["SGT"]
    SMWB = bufs["SMWB"]
    slab = bufs["slab"]
    arena_war = bufs.get("arena_war", ())

    def _stop(stage, deps):
        if FFN_STOP == stage:
            join = tp.op("vector", lambda e: e.engine_nop(), deps=tuple(deps))
            return [join] * NCC
        return None

    xi_ops = emit_ln(k, consts, cdeps, scr, X, x_deps, ARENA, tag=f"{tag}_ln0",
                     out_war=arena_war)
    r = _stop("ln0", xi_ops)
    if r: return r

    g1 = [sm["lnp"][:, 0 + cc:0 + cc + 1] for cc in range(NCC)]
    b1 = [sm["lnp"][:, 8 + cc:8 + cc + 1] for cc in range(NCC)]
    g2 = [sm["lnp"][:, 16 + cc:16 + cc + 1] for cc in range(NCC)]
    b2 = [sm["lnp"][:, 24 + cc:24 + cc + 1] for cc in range(NCC)]
    xln_ops = emit_ln(k, consts, cdeps, scr, ARENA, xi_ops, ACT1, gamma=g1, beta=b1,
                      tag=f"{tag}_ln1")
    r = _stop("ln1", xln_ops)
    if r: return r

    u_ops = {}

    def mk_sig(mi, nt, bank):
        sl = slice(nt * 512, (nt + 1) * 512)
        o1 = k.rd(bank, "scalar", lambda e, mi=mi, sl=sl, bank=bank: e.activation(
            out=ACT2[mi][:, sl], in_=k.pb[bank][:, 0:512], func=AF.Tanh, scale=0.5))
        o2 = tp.op("vector", lambda e, mi=mi, sl=sl: e.tensor_scalar(
            ACT2[mi][:, sl], ACT2[mi][:, sl], 0.5, 0.5, ALU.mult, ALU.add), deps=(o1,))
        u_ops[(mi, nt)] = o2
    wr_groups = emit_ccpass(k, slab, W["Wr"], ACT1, xln_ops, mk_sig)
    r = _stop("u", list(u_ops.values()))
    if r: return r

    gated_ops = {}

    def mk_gate(mi, nt, bank):
        sl = slice(nt * 512, (nt + 1) * 512)
        o = k.rd(bank, "vector", lambda e, mi=mi, sl=sl, bank=bank: e.tensor_mul(
            ACT2[mi][:, sl], ACT2[mi][:, sl], k.pb[bank][:, 0:512]), deps=())
        k.tp.ops[o]["force"] = k.tp.ops[o]["force"] + (u_ops[(mi, nt)],)
        gated_ops[(mi, nt)] = o
    wv_groups = emit_ccpass(k, slab, W["Wv"], ACT1, xln_ops, mk_gate)
    r = _stop("gated", list(gated_ops.values()))
    if r: return r

    att_ops = {}

    def mk_att(mi, nt, bank):
        sl = slice(nt * 512, (nt + 1) * 512)
        o = k.rd(bank, "vector", lambda e, mi=mi, sl=sl, bank=bank: e.tensor_add(
            ARENA[mi][:, sl], ARENA[mi][:, sl], k.pb[bank][:, 0:512]))
        att_ops[(mi, nt)] = o
    wo_groups = emit_ccpass(k, slab, W["Wo"], ACT2, list(gated_ops.values()), mk_att)
    r = _stop("att", list(att_ops.values()))
    if r: return r

    state_ops = {}

    def mk_state(mi, nt, bank):
        sl = slice(nt * 512, (nt + 1) * 512)
        o = k.rd(bank, "scalar", lambda e, mi=mi, sl=sl, bank=bank: e.copy(
            out=ACT2[mi][:, sl], in_=k.pb[bank][:, 0:512]), deps=tuple(wo_groups))
        state_ops[(mi, nt)] = o
    ws_groups = emit_ccpass(k, slab, W["Ws"], ACT1, xln_ops, mk_state)
    r = _stop("state", list(state_ops.values()))
    if r: return r

    xi2_ops = [None] * NCC
    for cc in range(NCC):
        xi2_ops[cc] = att_ops[(cc, 0)]
    xi2_all = list(att_ops.values())

    h_ops = emit_ln(k, consts, cdeps, scr, ARENA, xi2_all, ACT1, gamma=g2, beta=b2,
                    tag=f"{tag}_ln2", out_war=tuple(ws_groups))

    r = _stop("h", h_ops)
    if r: return r
    xadd_ops = []
    for cc in range(NCC):
        o = tp.op("gpsimd", lambda e, cc=cc: e.tensor_add(X[cc][:], X[cc][:], ARENA[cc][:]),
                  deps=tuple(h_ops) + (x_deps[cc],))
        xadd_ops.append(o)

    grow_ops = []
    for tc in range(16):
        tsl = slice(tc * 128, (tc + 1) * 128)
        bank = (tc % 4)
        calls = [(ACT1[cc][:, tsl], sm["packT"][cc][:]) for cc in range(NCC)]
        gmm = k.mm(bank, calls, deps=tuple(h_ops) + (cdeps["_packT"],), ncols=8)
        o1 = k.rd(bank, "scalar", lambda e, bank=bank: e.activation(
            out=SMW[:, 0:4], in_=k.pb[bank][:, 0:4], func=AF.Tanh, scale=0.5),
            deps=(grow_ops[-1],) if grow_ops else ())

        S = SMW
        psl = k.pb[bank]
        wsteps = [
            lambda e: e.tensor_scalar(S[:, 0:4], S[:, 0:4], 0.5, 0.5, ALU.mult, ALU.add),
            lambda e: e.tensor_mul(S[:, 4:7], S[:, 0:3], sm["capb3"][:, 0:3]),
            lambda e: e.tensor_single_scalar(S[:, 4:7], S[:, 4:7], S[:, 3:4], ALU.mult),
            lambda e, psl=psl: e.tensor_add(S[:, 4:7], S[:, 4:7], psl[:, 4:7]),
            lambda e: e.tensor_tensor(S[:, 7:8], S[:, 4:5], S[:, 5:6], ALU.is_ge),
            lambda e: e.tensor_tensor(S[:, 8:9], S[:, 4:5], S[:, 6:7], ALU.is_ge),
            lambda e: e.tensor_mul(S[:, 16:17], S[:, 7:8], S[:, 8:9]),
            lambda e: e.tensor_tensor(S[:, 7:8], S[:, 5:6], S[:, 4:5], ALU.is_gt),
            lambda e: e.tensor_tensor(S[:, 8:9], S[:, 5:6], S[:, 6:7], ALU.is_ge),
            lambda e: e.tensor_mul(S[:, 17:18], S[:, 7:8], S[:, 8:9]),
            lambda e: e.tensor_tensor(S[:, 7:8], S[:, 6:7], S[:, 4:5], ALU.is_gt),
            lambda e: e.tensor_tensor(S[:, 8:9], S[:, 6:7], S[:, 5:6], ALU.is_gt),
            lambda e: e.tensor_mul(S[:, 18:19], S[:, 7:8], S[:, 8:9]),
            lambda e: e.tensor_mul(S[:, 9:10], S[:, 16:17], S[:, 0:1]),
            lambda e: e.tensor_mul(S[:, 10:11], S[:, 17:18], S[:, 1:2]),
            lambda e: e.tensor_add(S[:, 9:10], S[:, 9:10], S[:, 10:11]),
            lambda e: e.tensor_mul(S[:, 10:11], S[:, 18:19], S[:, 2:3]),
            lambda e: e.tensor_add(S[:, 11:12], S[:, 9:10], S[:, 10:11]),
            lambda e: e.tensor_scalar_add(S[:, 12:13], S[:, 11:12], 1e-6),
            lambda e: e.reciprocal(S[:, 13:14], S[:, 12:13]),
            lambda e: e.tensor_mul(S[:, 12:13], S[:, 11:12], S[:, 13:14]),
            lambda e: e.tensor_mul(S[:, 20:21], S[:, 16:17], S[:, 12:13]),
            lambda e: e.tensor_mul(S[:, 21:22], S[:, 17:18], S[:, 12:13]),
            lambda e: e.tensor_mul(S[:, 22:23], S[:, 18:19], S[:, 12:13]),
        ]
        wids = k.vseq("vector", wsteps, deps=(o1, cdeps["_capb3"], k.pb_group[bank]))
        k.pb_readers[bank].append(wids[3])
        o2 = wids[-1]
        o3 = tp.op("vector", lambda e: e.tensor_copy(SMWB[:], SMW[:, 20:24]), force=(o2,))
        tb = 4 + (tc % 2)
        tr = k.transpose_into(tb, 0, SMWB[:], consts["ident_bf"][:], deps=(o3, cdeps["ident_bf"]))
        ev = k.rd(tb, "scalar", lambda e, tsl=tsl, tb=tb: e.copy(
            out=GROWS[0:4, tsl], in_=k.pb[tb].ap().bitcast(BF16)[0:4, 0:128]))
        grow_ops.append(ev)

    r = _stop("grows", grow_ops + xadd_ops)
    if r: return r
    gb_state = {"readers": ()}

    def build_gb(e_i):
        ops_e = []
        for nt in range(4):
            sl = slice(nt * 512, (nt + 1) * 512)
            bank = nt % 2
            g = k.mm(bank, [(consts["sel4_bf"][:, e_i * 128:(e_i + 1) * 128], GROWS[:, sl])],
                     deps=tuple(grow_ops) + (cdeps["sel4_bf"],) + gb_state["readers"])
            o = k.rd(bank, "vector", lambda e, sl=sl, bank=bank: e.tensor_copy(
                GB[0][:, sl], k.pb[bank][:, 0:512]))
            ops_e.append(o)
        gb_state["readers"] = ()
        return ops_e

    g2bf = [ARENA[i].ap() for i in range(NCC)]
    slab2 = bufs["slab2"]
    g2_ops = {}
    sg_prev = {}
    for mi in range(NCC):
        w2bf, c2, b2i = slab2.load(lambda mi=mi: cc_slab_ap(W["W2"], mi))
        w1bf, c1, b1i = slab.load(lambda mi=mi: cc_slab_ap(W["W1"], mi))
        for nt in range(4):
            sl = slice(nt * 512, (nt + 1) * 512)
            bank = (mi * 4 + nt) % 4
            sgb = SGT[nt % 2]
            calls2 = [(w2bf[:, cc * 128:(cc + 1) * 128], ACT2[cc][:, sl]) for cc in range(NCC)]
            gmm2 = k.mm(bank, calls2, deps=(c2,) + tuple(state_ops.values()))
            slab2.used(b2i, gmm2)
            so1 = k.rd(bank, "scalar", lambda e, bank=bank, sgb=sgb: e.activation(
                out=sgb[:], in_=k.pb[bank][:, 0:512], func=AF.Tanh, scale=0.5),
                deps=(sg_prev.get(nt % 2),))
            so2 = tp.op("vector", lambda e, sgb=sgb: e.tensor_scalar(
                sgb[:], sgb[:], 0.5, 0.5, ALU.mult, ALU.add), deps=(so1,))
            bank1 = 4 + (nt % 2)
            calls1 = [(w1bf[:, cc * 128:(cc + 1) * 128], ACT1[cc][:, sl]) for cc in range(NCC)]
            gmm1 = k.mm(bank1, calls1, deps=(c1,) + tuple(h_ops))
            slab.used(b1i, gmm1)
            o = k.rd(bank1, "vector", lambda e, mi=mi, sl=sl, bank1=bank1, sgb=sgb: e.tensor_mul(
                g2bf[mi][:, sl], k.pb[bank1][:, 0:512], sgb[:]), deps=(xadd_ops[mi],))
            k.tp.ops[o]["force"] = k.tp.ops[o]["force"] + (so2,)
            g2_ops[(mi, nt)] = o
            sg_prev[nt % 2] = o

    r = _stop("g2", list(g2_ops.values()) + xadd_ops)
    if r: return r
    gb2_ops = build_gb(2)
    w3_done = []
    w3_gbr = []

    def mk_w3(mi, nt, bank):
        sl = slice(nt * 512, (nt + 1) * 512)
        b = (mi * 4 + nt) % 2
        o1 = k.rd(bank, "vector", lambda e, sl=sl, bank=bank, b=b: e.tensor_mul(
            GS[b][:], k.pb[bank][:, 0:512], GB[0][:, sl]),
            deps=tuple(gb2_ops) + (w3_done[-2] if len(w3_done) >= 2 else None,))
        o2 = tp.op("gpsimd", lambda e, mi=mi, sl=sl, b=b: e.tensor_add(
            X[mi][:, sl], X[mi][:, sl], GS[b][:]), deps=(o1, xadd_ops[mi]))
        w3_done.append(o2)
        w3_gbr.append(o1)
    w3_groups = emit_ccpass(k, slab, W["W3"], g2bf, list(g2_ops.values()), mk_w3)
    gb_state["readers"] = tuple(w3_gbr)

    kslab = bufs["kslab"]
    vslab = bufs["vslab"]
    HRbf = ARENA

    def hr_ap(hc, w):
        i, j = hc // 4, hc % 4
        return HRbf[i][:, j * 512:j * 512 + w]

    r = _stop("w3", w3_done)
    if r: return r
    last_expert_ops = list(w3_done)
    for e_i in range(2):
        gbe_ops = build_gb(e_i)
        gbe_readers = []
        for tt in range(4):
            sl = slice(tt * 512, (tt + 1) * 512)
            hr_ready = []
            for hc in range(NHC):
                kbf, kc, kb = kslab.load(
                    lambda e_i=e_i, hc=hc: W["K_rwkv"][e_i].rearrange("(cc p) h -> p cc h", p=128)[:, :, hc * 128:(hc + 1) * 128])
                bank = hc % 2
                calls = [(kbf[:, cc * 128:(cc + 1) * 128], ACT1[cc][:, sl]) for cc in range(NCC)]
                gmm = k.mm(bank, calls, deps=(kc,) + tuple(h_ops) + tuple(last_expert_ops))
                kslab.used(kb, gmm)
                rb = RL[hc % 2]
                o1 = k.rd(bank, "scalar", lambda e, rb=rb, bank=bank: e.activation(
                    out=rb[:], in_=k.pb[bank][:, 0:512], func=AF.Relu),
                    deps=(hr_ready[hc - 2] if hc >= 2 else None,))
                o2 = tp.op("gpsimd", lambda e, hc=hc, rb=rb: e.tensor_mul(
                    hr_ap(hc, 512), rb[:], rb[:]), deps=(o1,))
                hr_ready.append(o2)
            vgroups = [None] * NCC
            for hc in range(NHC):
                vbf, vc, vb = vslab.load(
                    lambda e_i=e_i, hc=hc: W["V_rwkv"][e_i][hc * 128:(hc + 1) * 128, :])
                for mi in range(NCC):
                    gmm = k.mm1(mi, vbf[:, mi * 128:(mi + 1) * 128], hr_ap(hc, 512),
                                start=(hc == 0), stop=(hc == NHC - 1),
                                deps=(vc, hr_ready[hc]))
                    vslab.used(vb, gmm)
                    vgroups[mi] = gmm
            ev_ops = []
            for mi in range(NCC):
                b = mi % 2
                gbap = GB[0][:, sl]
                o1 = k.rd(mi, "vector", lambda e, mi=mi, b=b, gbap=gbap: e.tensor_mul(
                    GS[b][:], k.pb[mi][:, 0:512], gbap),
                    deps=tuple(gbe_ops) + (ev_ops[-2][1] if len(ev_ops) >= 2 else None,))
                gbe_readers.append(o1)
                o2 = tp.op("gpsimd", lambda e, mi=mi, b=b, sl=sl: e.tensor_add(
                    X[mi][:, sl], X[mi][:, sl], GS[b][:]), deps=(o1,))
                ev_ops.append((o1, o2))
            last_expert_ops = [o2 for (o1, o2) in ev_ops]
        gb_state["readers"] = tuple(gbe_readers)
        r = _stop(f"e{e_i}", last_expert_ops)
        if r: return r

    new_deps = [last_expert_ops[cc] if cc < len(last_expert_ops) else xadd_ops[cc] for cc in range(NCC)]
    bufs["arena_war"] = tuple(last_expert_ops)
    return new_deps



def alloc_bufs(k):
    b = {}
    b["X"] = [k.sb("X", [128, T], F32) for _ in range(NCC)]
    b["ARENA"] = [k.sb("AR", [128, T], BF16) for _ in range(NCC)]
    b["ACT1"] = [k.sb("A1", [128, T], BF16) for _ in range(NCC)]
    b["ACT2"] = [k.sb("A2", [128, T], BF16) for _ in range(NCC)]
    b["GB"] = [k.sb("GB", [128, T], BF16)]
    b["GS"] = [k.sb("GS", [128, 512], F32) for _ in range(2)]
    b["GROWS"] = k.sb("GROWS", [4, T], BF16)
    b["SMW"] = k.sb("SMW", [128, 32], F32)
    b["SMWB"] = k.sb("SMWB", [128, 4], BF16)
    b["RL"] = [k.sb("RL", [128, 512], BF16) for _ in range(2)]
    b["rrow"] = k.sb("rrow", [1, 512], BF16)
    b["bbias"] = k.sb("bbias", [128, 8], F32)

    b["SGT"] = [k.sb("SGT", [128, 512], BF16) for _ in range(2)]
    b["slab"] = Slab(k, "slA")
    b["slab2"] = Slab(k, "slB")
    b["kslab"] = b["slab"]
    b["vslab"] = b["slab2"]
    return b


def load_smalls(k, consts, cdeps, ins, bufs):
    tp = k.tp
    sm = {}
    pstage = bufs["GS"][0].ap()[:, 0:8]
    lnp = k.sb("lnp", [128, 32], F32)
    deps = []
    for i, nm in enumerate(("L0_ffn_ln1_w", "L0_ffn_ln1_b", "L0_ffn_ln2_w", "L0_ffn_ln2_b")):
        for cc in range(NCC):
            d = tp.dma("gpsimd", lambda e, i=i, cc=cc, nm=nm: e.dma_start(
                out=lnp[:, i * 8 + cc:i * 8 + cc + 1],
                in_=ins[nm].rearrange("(cc p) -> cc p", p=128)[cc:cc + 1, :].rearrange("o p -> p o")))
            deps.append(d)
    sm["lnp"] = lnp

    packs = [k.sb("packT", [128, 8], BF16) for _ in range(NCC)]
    pack_deps = []
    prev_cast = None
    for cc in range(NCC):
        csl = slice(cc * 128, (cc + 1) * 128)
        dd = []
        dd.append(tp.dma("gpsimd", lambda e, csl=csl: e.dma_start(
            out=pstage[:, 0:1], in_=ins["L0_ffn_conf_rwkv"][0:1, csl].rearrange("o p -> p o")), deps=(prev_cast,)))
        dd.append(tp.dma("gpsimd", lambda e, csl=csl: e.dma_start(
            out=pstage[:, 1:2], in_=ins["L0_ffn_conf_rwkv"][1:2, csl].rearrange("o p -> p o")), deps=(prev_cast,)))
        dd.append(tp.dma("gpsimd", lambda e, csl=csl: e.dma_start(
            out=pstage[:, 2:3], in_=ins["L0_ffn_conf_trans"].rearrange("(a p) -> a p", a=NCC)[csl.start // 128:csl.start // 128 + 1, :].rearrange("o p -> p o")), deps=(prev_cast,)))
        dd.append(tp.dma("gpsimd", lambda e, csl=csl: e.dma_start(
            out=pstage[:, 3:4], in_=ins["L0_ffn_w_diff"].rearrange("(a p) -> a p", a=NCC)[csl.start // 128:csl.start // 128 + 1, :].rearrange("o p -> p o")), deps=(prev_cast,)))
        dd.append(tp.dma("gpsimd", lambda e, csl=csl: e.dma_start(
            out=pstage[:, 4:7], in_=ins["L0_ffn_W_aff"][csl, :]), deps=(prev_cast,)))
        mz = tp.op("vector", lambda e: e.memset(pstage[:, 7:8], 0.0), deps=(prev_cast,))
        cst = tp.op("gpsimd", lambda e, cc=cc: e.tensor_copy(packs[cc][:], pstage),
                    deps=tuple(dd) + (mz,))
        prev_cast = cst
        pack_deps.append(cst)
    sm["packT"] = packs
    bufs["packs"] = packs
    cap_row = None
    capb3 = k.sb("capb3", [128, 4], F32)
    d = tp.dma("gpsimd", lambda e: e.dma_start(out=capb3[0:1, 0:3], in_=ins["L0_ffn_capital_shares"].rearrange("(o c) -> o c", o=1)))
    g = k.mm(7, [(consts["ones_row_f32"][0:1, 0:128], capb3[0:1, 0:3])],
             deps=(d, cdeps["ones_row_f32"]), ncols=3)
    cb = k.rd(7, "vector", lambda e: e.tensor_copy(capb3[:, 0:3], k.pb[7][:, 0:3]), deps=(d,))
    k.tp.ops[cb]["force"] = k.tp.ops[cb]["force"] + (g,)
    sm["capb3"] = capb3
    bfc = bufs["SMW"].ap()[:, 16:24]
    for cc in range(NCC):
        deps.append(tp.dma("gpsimd", lambda e, cc=cc: e.dma_start(
            out=bfc[:, cc:cc + 1] if hasattr(bfc, "dtype") else bfc[:, cc:cc + 1],
            in_=ins["L0_attn_a_b_fc"].rearrange("(a p) -> a p", a=NCC)[cc:cc + 1, :].rearrange("o p -> p o"))))
    sm["bfc"] = bfc
    cdeps["_lnp"] = deps[-1]
    cdeps["_lnp_all"] = tuple(deps)
    cdeps["_packT"] = pack_deps[-1]
    cdeps["_packT_all"] = tuple(pack_deps)
    cdeps["_capb3"] = cb
    bufs["capb3x"] = [capb3]
    return sm


def emit_load_x(k, consts, cdeps, xin, X, stage):
    tp = k.tp
    deps = [[] for _ in range(NCC)]
    prev_ev = {}
    for tc in range(16):
        b = tc % 2
        war = tuple(prev_ev.get((tc - 2, cc)) for cc in range(NCC)) if tc >= 2 else ()
        d = tp.dma("sync", lambda e, tc=tc, b=b: e.dma_start(
            out=stage[b][:], in_=xin[tc * 128:(tc + 1) * 128, :]), deps=war, stream=f"st{b}")
        for cc in range(NCC):
            bank = cc
            tr = k.transpose_into(bank, 0, stage[b][:, cc * 128:(cc + 1) * 128],
                                  consts["ident_f32"][:], deps=(d, cdeps["ident_f32"]))
            ev = k.rd(bank, "scalar", lambda e, cc=cc, tc=tc, bank=bank: e.copy(
                out=X[cc][:, tc * 128:(tc + 1) * 128], in_=k.pb[bank][:, 0:128]))
            prev_ev[(tc, cc)] = ev
            deps[cc].append(ev)
    lastev = {0: prev_ev[(14, NCC - 1)], 1: prev_ev[(15, NCC - 1)]}
    return [deps[cc][-1] for cc in range(NCC)], lastev


def emit_store_x(k, consts, cdeps, X, x_deps, yout, stage, stage_war=()):
    tp = k.tp
    st_ops = []
    prev_dma = {}
    for tc in range(16):
        b = tc % 2
        evs = []
        for cc in range(NCC):
            bank = cc
            tr = k.transpose_into(bank, 0, X[cc][:, tc * 128:(tc + 1) * 128],
                                  consts["ident_f32"][:],
                                  deps=(x_deps[cc], cdeps["ident_f32"]))
            ev = k.rd(bank, "scalar", lambda e, cc=cc, bank=bank, b=b: e.copy(
                out=stage[b][:, cc * 128:(cc + 1) * 128], in_=k.pb[bank][:, 0:128]),
                deps=(prev_dma.get(tc - 2),) + tuple(stage_war))
            evs.append(ev)
        d = tp.dma("gpsimd", lambda e, tc=tc, b=b: e.dma_start(
            out=yout[tc * 128:(tc + 1) * 128, :], in_=stage[b][:]), deps=tuple(evs))
        prev_dma[tc] = d
        st_ops.append(d)
    return st_ops



IN_SPECS = {
    "L0_attn_a_x": [T, C], "L0_attn_a_w_qkv": [C, 3 * C], "L0_attn_a_w_conv": [16, 1, 13, 1],
    "L0_attn_a_w_fc": [C, C], "L0_attn_a_b_fc": [C],
    "L0_attn_b_Wq": [C, C], "L0_attn_b_Wk": [C, C], "L0_attn_b_Wv": [C, C], "L0_attn_b_Wo": [C, C],
    "L0_ffn_v_first": [4, T, C], "L0_ffn_capital_shares": [3],
    "L0_ffn_ln1_w": [C], "L0_ffn_ln1_b": [C], "L0_ffn_ln2_w": [C], "L0_ffn_ln2_b": [C],
    "L0_ffn_Wr": [C, C], "L0_ffn_Wv": [C, C], "L0_ffn_Wo": [C, C], "L0_ffn_Ws": [C, C],
    "L0_ffn_K_rwkv": [2, C, HID], "L0_ffn_V_rwkv": [2, HID, C], "L0_ffn_conf_rwkv": [2, C],
    "L0_ffn_W1": [C, C], "L0_ffn_W2": [C, C], "L0_ffn_W3": [C, C],
    "L0_ffn_conf_trans": [C], "L0_ffn_w_diff": [C], "L0_ffn_W_aff": [C, 3],
    "L0_ffn_W_bridge": [C, C],
}


def build_program(parts=("attn_a", "ffn1", "attn_b", "ffn2")):
    nc = bass.Bass("TRN2", target_bir_lowering=False,
                   detect_race_conditions=RACE_DETECT)
    ins = {}
    for nm, shp in IN_SPECS.items():
        ins[nm] = nc.dram_tensor(nm, shp, F32, kind="ExternalInput").ap()
    yout = nc.dram_tensor("y", [T, C], F32, kind="ExternalOutput").ap()

    with contextlib.ExitStack() as es:
        k = K(nc, es)
        carrs = make_const_arrays()
        consts, cdeps = load_consts(k, carrs)
        bufs = alloc_bufs(k)
        scr = alloc_ln_scratch(k)
        sm = load_smalls(k, consts, cdeps, ins, bufs)
        globals()["LAST_BUFS"] = bufs
        globals()["LAST_SCR"] = scr
        X = bufs["X"]
        stage = [bufs["slab"].raw[0], bufs["slab2"].raw[0]]
        x_deps, lastev = emit_load_x(k, consts, cdeps, ins["L0_attn_a_x"], X, stage)
        bufs["slab"].prev_cast = [lastev[0]]
        bufs["slab2"].prev_cast = [lastev[1]]

        Wffn = {kk: ins["L0_ffn_" + kk] for kk in
                ("Wr", "Wv", "Wo", "Ws", "K_rwkv", "V_rwkv", "W1", "W2", "W3")}
        for part in parts:
            if part in ("ffn1", "ffn2"):
                x_deps = emit_ffn(k, consts, cdeps, scr, Wffn, sm, X, x_deps, bufs, tag=part)
            elif part == "attn_a":
                x_deps = emit_attn_a(k, consts, cdeps, scr, ins, sm, X, x_deps, bufs)
            elif part == "attn_b":
                x_deps = emit_attn_b(k, consts, cdeps, scr, ins, sm, X, x_deps, bufs)
        dump_outs = {}
        for dn in DUMPS:
            bl = bufs[dn] if dn in bufs else scr[dn]
            if not isinstance(bl, list):
                bl = [bl]
            dt_ = bl[0].ap().dtype
            dd = nc.dram_tensor("dbg_" + dn, [len(bl) * 128, bl[0].ap().shape[-1]], dt_,
                                kind="ExternalOutput").ap()
            for i3, b3 in enumerate(bl):
                k.tp.dma("gpsimd", lambda e, i3=i3, b3=b3, dd=dd: e.dma_start(
                    out=dd[i3 * 128:(i3 + 1) * 128 - (128 - b3.ap().shape[0])], in_=b3[:]),
                    deps=tuple(x_deps))
        st = emit_store_x(k, consts, cdeps, X, x_deps, yout, stage,
                          stage_war=tuple(p for p in bufs["slab"].prev_cast + bufs["slab2"].prev_cast
                                          if p is not None))
        k.tp.op("gpsimd", lambda e: None, deps=tuple(st))
        block = es.enter_context(nc.Block())
        k.tp.emit(nc, block, es)
    return nc


def kernel(**inputs):
    x_full = np.ascontiguousarray(inputs["L0_attn_a_x"], dtype=np.float32)
    nc = build_program()
    in_maps = []
    for b in range(N_CORES):
        m = {}
        for nm in IN_SPECS:
            if nm == "L0_attn_a_x":
                m[nm] = np.ascontiguousarray(x_full[b])
            else:
                m[nm] = np.ascontiguousarray(inputs[nm], dtype=np.float32)
        in_maps.append(m)
    res = run_bass_kernel_spmd(nc, in_maps, core_ids=list(range(N_CORES)))
    out = np.stack([res.results[b]["y"] for b in range(N_CORES)], axis=0)
    return out.astype(np.float32)


class PoolAlloc:

    def __init__(self, slots, init_war):
        self.slots = slots
        self.free = list(range(len(slots)))
        self.readers = {i: list(init_war) for i in range(len(slots))}

    def alloc(self, below=None):
        for j, i in enumerate(self.free):
            if below is None or i < below:
                return self.free.pop(j)
        raise RuntimeError("pool exhausted")

    def ap(self, i):
        return self.slots[i]

    def war(self, i):
        return tuple(self.readers[i])

    def wrote(self, i, opid):
        self.readers[i] = [opid]

    def read(self, i, opid):
        self.readers[i].append(opid)

    def release(self, i):
        self.free.append(i)


def emit_attn_a(k, consts, cdeps, scr, ins, sm, X, x_deps, bufs, tag="ata"):
    tp = k.tp
    nc = k.nc
    ARENA, ACT1, ACT2, GB, GS = bufs["ARENA"], bufs["ACT1"], bufs["ACT2"], bufs["GB"], bufs["GS"]
    SMW, SMWB, bbias, RL, SGT = bufs["SMW"], bufs["SMWB"], bufs["bbias"], bufs["RL"], bufs["SGT"]
    slab, slab2 = bufs["slab"], bufs["slab2"]
    wq = ins["L0_attn_a_w_qkv"]
    arena_war = list(bufs.get("arena_war", ()))
    a2war = list(bufs.get("act2_war", ()))
    gb_war = list(bufs.get("gb_war", ()))

    xa_ops = emit_ln(k, consts, cdeps, scr, X, x_deps, ACT1, tag=f"{tag}_ln",
                     out_war=bufs.get("act1_war", ()))
    if ATA_STOP == "ln":
        return list(x_deps) + list(xa_ops)

    slots = ([ARENA[6].ap()[:, i * 512:(i + 1) * 512] for i in range(4)]
             + [scr["ln_cast"][i][:] for i in range(2)]
             + [scr["ln_sq"][0][:]]
             + [GS[i].ap().bitcast(BF16)[:, j * 512:(j + 1) * 512] for i in range(2) for j in range(2)])
    PA = PoolAlloc(slots, arena_war + list(scr.get("ln_gate_row", ()))
                   + [d for d in (cdeps.get("_packT"), cdeps.get("_capb3"), cdeps.get("_lnp"))
                      if d is not None])
    iI = {}
    for nm in ("i7_bf", "i15_bf", "i13_bf"):
        i = PA.alloc()
        d = tp.dma("gpsimd", lambda e, nm=nm, i=i: e.dma_start(out=PA.ap(i), in_=k.cdram[nm].ap()),
                   deps=PA.war(i))
        PA.wrote(i, d)
        iI[nm] = i

    qlT = GB[0]
    klT = [RL[0], RL[1], SGT[0], SGT[1]]

    def klT_ap(hp):
        return klT[hp // 2][:, (hp % 2) * 256:(hp % 2) * 256 + 256]

    vpack = ARENA[7]

    def a_ap(h):
        return ACT2[h // 2].ap()[:, (h % 2) * 1024:(h % 2) * 1024 + 512]

    def aT_ap(h):
        return ACT2[h // 2].ap()[:, (h % 2) * 1024 + 512:(h % 2) * 1024 + 1024]

    cc_in = nc.dram_tensor(k.name("ccin"), [1, 1], F32, kind="Internal")
    cc_out = nc.dram_tensor(k.name("ccout"), [1, 1], F32, kind="Internal")
    ccsem = k.es.enter_context(nc.semaphore(k.name("ccsem")))

    qk_rd = {0: list(arena_war), 1: list(arena_war)}
    ql_rd = list(gb_war) + [cdeps.get("_packT"), cdeps.get("_capb3")]
    kl_rd = []
    a_rd = {h: list(a2war) for h in range(H)}
    gs1row_rd = []
    smw_rd = [cdeps.get("_lnp")]
    bias_rd = []
    vp_rd = list(arena_war)

    def qkv_project(hp, which, slot, scale, slabX, extra_rd):
        base = {"q": 0, "k": C, "v": 2 * C}[which] + hp * 128
        wbf, cst, bidx = slabX.load(lambda base=base: wq.rearrange(
            "(cc p) o -> p cc o", p=128)[:, :, base:base + 128])
        done = []
        first_rd = tuple(extra_rd)
        for nt in range(4):
            sl = slice(nt * 512, (nt + 1) * 512)
            bank = nt % 2
            calls = [(wbf[:, ccx * 128:(ccx + 1) * 128], ACT1[ccx][:, sl]) for ccx in range(NCC)]
            gmm = k.mm(bank, calls, deps=(cst,) + tuple(xa_ops))
            slabX.used(bidx, gmm)
            evo = k.rd(bank, "scalar", lambda e, slot=slot, sl=sl, bank=bank, scale=scale: e.activation(
                out=slot[:, sl], in_=k.pb[bank][:, 0:512], func=AF.Copy, scale=scale),
                deps=first_rd)
            first_rd = ()
            done.append(evo)
        return done

    def landmark_pool(src_slot, dst_ap, src_dep, war):
        acc = GS[0].ap()[:, 0:256]
        v8 = src_slot.ap().rearrange("p (m j) -> p j m", j=8)
        steps = [lambda e: e.tensor_add(acc, v8[:, 0, :], v8[:, 1, :])]
        for j in range(2, 8):
            steps.append(lambda e, j=j: e.tensor_add(acc, acc, v8[:, j, :]))
        steps.append(lambda e: e.tensor_scalar_mul(dst_ap, acc, 1.0 / 8.0))
        return k.vseq("vector", steps, deps=(src_dep,) + tuple(war))

    def norm_max(ap_, dep_, dst_col, width, slot):
        nonlocal smw_rd
        sq_out = PA.ap(slot)
        ops = []
        seq = None
        nblk = width // 512 if width >= 512 else 1
        for i3 in range(nblk):
            w3 = min(512, width)
            sq_o = tp.op("scalar", lambda e, ap_=ap_, i3=i3, w3=w3, sq_out=sq_out: e.activation(
                out=sq_out[:, 0:w3], in_=ap_[:, i3 * 512:i3 * 512 + w3], func=AF.Square),
                deps=(dep_, seq) + PA.war(slot))
            PA.wrote(slot, sq_o)
            gmm = k.mm(4, [(consts["headsel_bf"][:], sq_out[:, 0:w3])],
                       deps=(sq_o, cdeps["headsel_bf"]), out_ap=k.pb[4][0:2, 0:w3])
            PA.read(slot, gmm)
            dst = SMW[:, dst_col:dst_col + 1] if i3 == 0 else SMW[:, 30:31]
            red = k.rd(4, "vector", lambda e, dst=dst, w3=w3: e.reduce_max(
                dst[0:2, :], k.pb[4][0:2, 0:w3], AX.X), deps=tuple(smw_rd))
            smw_rd = []
            if i3 > 0:
                red = tp.op("vector", lambda e, dst_col=dst_col: e.tensor_max(
                    SMW[:, dst_col:dst_col + 1][0:2, :], SMW[:, dst_col:dst_col + 1][0:2, :],
                    SMW[:, 30:31][0:2, :]), force=(red,))
            seq = red
            ops.append(sq_o)
        smw_rd = [seq]
        return seq, ops

    def bias_broadcast(colA, colB, dst_col):
        nonlocal smw_rd, bias_rd
        cb = k.vseq("vector", [
            lambda e: e.tensor_add(SMW[:, 30:31][0:2, :], SMW[:, colA:colA + 1][0:2, :],
                                   SMW[:, colB:colB + 1][0:2, :]),
            lambda e: e.tensor_scalar_mul(SMW[:, 30:31][0:2, :], SMW[:, 30:31][0:2, :], -0.5),
            lambda e: e.tensor_copy(SMWB[0:2, 0:1], SMW[:, 30:31][0:2, :]),
        ], deps=tuple(smw_rd))[-1]
        smw_rd = [cb]
        outs = []
        for hh in range(2):
            gmm = k.mm(4, [(consts["rowsel_bf"][:, hh * 128:(hh + 1) * 128], SMWB[0:2, 0:1])],
                       deps=(cb, cdeps["rowsel_bf"]) + tuple(bias_rd), ncols=1)
            bias_rd = []
            outs.append(k.rd(4, "vector", lambda e, hh=hh, dst_col=dst_col: e.tensor_copy(
                bbias[:, dst_col + hh:dst_col + hh + 1], k.pb[4][:, 0:1])))
        return outs

    colmax_chain = None
    for hp in range(ATA_P1):
        qd = qkv_project(hp, "q", ARENA[0], 0.125, slab, qk_rd[0])
        kd = qkv_project(hp, "k", ARENA[1], 1.0, slab2, qk_rd[1])
        qlap = qlT.ap()[:, hp * 256:hp * 256 + 256]
        po1 = landmark_pool(ARENA[0], qlap, qd[-1], ql_rd)
        ql_rd = []
        po2 = landmark_pool(ARENA[1], klT_ap(hp), kd[-1], tuple(kl_rd) + (po1[-1],))
        kl_rd = []
        qk_rd[0] = [po1[-1]]
        qk_rd[1] = [po2[-1]]
        ql_done, kl_done = po1[-1], po2[-1]
        sqslot = PA.alloc(below=7)
        n1, sq1 = norm_max(qlap, ql_done, 26, 256, sqslot)
        n2, sq2 = norm_max(klT_ap(hp), kl_done, 27, 256, sqslot)
        PA.release(sqslot)
        b_ops = bias_broadcast(26, 27, 0)
        for hh in range(2):
            h = 2 * hp + hh
            r0 = 64 * hh
            g = None
            for mi in range(2):
                g = k.mm1(5, qlT.ap()[r0:r0 + 64, hp * 256 + mi * 128:hp * 256 + (mi + 1) * 128],
                          klT_ap(hp)[r0:r0 + 64, 0:256],
                          start=(mi == 0), stop=(mi == 1),
                          deps=(ql_done, kl_done),
                          out_ap=k.pb[5][:, mi * 256:(mi + 1) * 256])
            ql_rd.append(g)
            kl_rd.append(g)
            exps = []
            for mi in range(2):
                exps.append(k.rd(5, "scalar", lambda e, mi=mi, hh=hh, h=h: e.activation(
                    out=a_ap(h)[:, mi * 256:(mi + 1) * 256], in_=k.pb[5][:, mi * 256:(mi + 1) * 256],
                    func=AF.Exp, bias=bbias[:, hh:hh + 1],
                    accum_out=SMW[:, 28 + mi:29 + mi]),
                    deps=(b_ops[hh],) + tuple(a_rd[h]) + tuple(smw_rd)))
            a_rd[h] = []
            bias_rd.extend(exps)
            rs = k.vseq("vector", [
                lambda e: e.tensor_add(SMW[:, 28:29], SMW[:, 28:29], SMW[:, 29:30]),
                lambda e: e.reciprocal(SMW[:, 29:30], SMW[:, 28:29]),
            ], deps=tuple(exps))
            smw_rd = [rs[-1]]
            nrm = tp.op("scalar", lambda e, h=h: e.activation(
                out=a_ap(h), in_=a_ap(h), func=AF.Copy, scale=SMW[:, 29:30]),
                deps=(exps[-1], rs[-1]))
            smw_rd.append(nrm)
            gsum = k.mm(4, [(consts["ones_col_bf"][:], a_ap(h))], deps=(nrm, cdeps["ones_col_bf"]),
                        out_ap=k.pb[4][0:1, 0:512])
            a_rd[h].append(gsum)
            def _csr(e):
                e.tensor_copy(GS[0].ap()[0:1, 0:512], k.pb[4][0:1, 0:512])
                return e.tensor_add(GS[0].ap()[0:1, 0:256], GS[0].ap()[0:1, 0:256],
                                    GS[0].ap()[0:1, 256:512])
            csr = k.rd(4, "vector", _csr, deps=tuple(gs1row_rd))
            red = tp.op("vector", lambda e: e.reduce_max(
                SMW[:, 30:31][0:1, :], GS[0].ap()[0:1, 0:256], AX.X), force=(csr,))
            gs1row_rd = [red]
            if colmax_chain is None:
                colmax_chain = tp.op("vector", lambda e: e.tensor_copy(
                    SMW[:, 14:15][0:1, :], SMW[:, 30:31][0:1, :]), force=(red,))
            else:
                colmax_chain = tp.op("vector", lambda e: e.tensor_max(
                    SMW[:, 14:15][0:1, :], SMW[:, 14:15][0:1, :], SMW[:, 30:31][0:1, :]),
                    force=(red, colmax_chain))
            trs = []
            for tb, pairs in ((6, ((0, 0), (0, 1))), (7, ((1, 0), (1, 1)))):
                for pi, (bi, bj) in enumerate(pairs):
                    t = k.transpose_into(tb, pi * 128,
                                         a_ap(h)[:, bj * 256 + bi * 128:bj * 256 + (bi + 1) * 128],
                                         consts["ident_bf"][:],
                                         deps=(nrm, cdeps["ident_bf"]), first=(pi == 0))
                    a_rd[h].append(t)
                ev = k.rd(tb, "scalar", lambda e, tb=tb, h=h: e.copy(
                    out=aT_ap(h)[:, (tb - 6) * 256:(tb - 6) * 256 + 256],
                    in_=k.pb[tb].ap().bitcast(BF16)[:, 0:256]),
                    deps=tuple(a_rd[h][-1:]))
                trs.append(ev)
            a_rd[h].extend(trs)

    if ATA_STOP == "pass1":
        return list(x_deps) + [colmax_chain]
    dcc1 = tp.dma("gpsimd", lambda e: e.dma_start(out=cc_in.ap(), in_=SMW[0:1, 14:15]),
                  deps=(colmax_chain,))

    def _cc(e):
        if ATA_LOCAL:
            e.dma_start(out=cc_out.ap(), in_=cc_in.ap()).then_inc(ccsem, 16)
            return None
        e.collective_compute("AllReduce", ALU.max, replica_groups=[list(range(N_CORES))],
                             ins=[cc_in.ap()], outs=[cc_out.ap()]).then_inc(ccsem, 1)
        return None
    ccop = tp.op("gpsimd", _cc, deps=(dcc1,))

    def _ccback(e):
        e.wait_ge(ccsem, 16 if ATA_LOCAL else 1)
        return e.dma_start(out=SMW[0:1, 30:31], in_=cc_out.ap())
    ccb = tp.dma("gpsimd", _ccback, deps=())
    sv = k.vseq("vector", [
        lambda e: e.reciprocal(SMW[:, 31:32][0:1, :], SMW[:, 30:31][0:1, :]),
        lambda e: e.tensor_copy(SMWB[0:1, 1:2], SMW[:, 31:32][0:1, :]),
    ], deps=(ccb,))[-1]
    gmm = k.mm(4, [(consts["ones_row_bf"][0:1, 0:128], SMWB[0:1, 1:2])],
               deps=(sv,) + tuple(bias_rd), ncols=1)
    bias_rd = []
    sinv_op = k.rd(4, "vector", lambda e: e.tensor_copy(bbias[:, 6:7], k.pb[4][:, 0:1]))
    if ATA_STOP == "cc":
        return list(x_deps) + [sinv_op]

    fc_rd = list(a2war)
    a23_rd = []
    a45_rd = []
    for hp in range(NCC):
        qd = qkv_project(hp, "q", ARENA[0], 0.125, slab, qk_rd[0])
        kd = qkv_project(hp, "k", ARENA[1], 1.0, slab2, qk_rd[1])
        qk_rd[0] = []
        qk_rd[1] = []
        base = 2 * C + hp * 128
        vbf, vcst, vbidx = slab.load(lambda base=base: wq.rearrange(
            "(cc p) o -> p cc o", p=128)[:, :, base:base + 128])
        vdone = []
        for tc in range(16):
            calls = [(ACT1[ccx][:, tc * 128:(tc + 1) * 128], vbf[:, ccx * 128:(ccx + 1) * 128])
                     for ccx in range(NCC)]
            gmmv = k.mm(2, calls, deps=(vcst,) + tuple(xa_ops), out_ap=k.pb[2][:, 0:128])
            slab.used(vbidx, gmmv)
            ev = k.rd(2, "scalar", lambda e, tc=tc: e.copy(
                out=vpack.ap()[:, tc * 128:(tc + 1) * 128], in_=k.pb[2][:, 0:128]),
                deps=tuple(vp_rd))
            vp_rd = []
            vdone.append(ev)
        sqslot = PA.alloc()
        nq, sqo1 = norm_max(ARENA[0].ap(), qd[-1], 24, 2048, sqslot)
        nk, sqo2 = norm_max(ARENA[1].ap(), kd[-1], 25, 2048, sqslot)
        nql, sqo3 = norm_max(qlT.ap()[:, hp * 256:hp * 256 + 256], sinv_op, 26, 256, sqslot)
        nkl, sqo4 = norm_max(klT_ap(hp), sinv_op, 27, 256, sqslot)
        PA.release(sqslot)
        bM1 = bias_broadcast(24, 27, 0)
        bM3 = bias_broadcast(26, 25, 2)
        if ATA_P2 == "qkv":
            qk_rd[0] = qd[-1:]
            qk_rd[1] = kd[-1:]
            vp_rd = vdone[-1:]
            continue
        zslots = {}
        for hh in range(2):
            h = 2 * hp + hh
            pz = PA.alloc()
            pzT = PA.alloc()
            z0 = tp.op("scalar", lambda e, h=h, pz=pz: e.activation(
                out=PA.ap(pz), in_=aT_ap(h), func=AF.Copy, scale=bbias[:, 6:7]),
                deps=(sinv_op,) + PA.war(pz) + tuple(a_rd[h]))
            PA.wrote(pz, z0)
            z0T = tp.op("scalar", lambda e, h=h, pzT=pzT: e.activation(
                out=PA.ap(pzT), in_=a_ap(h), func=AF.Copy, scale=bbias[:, 6:7]),
                deps=(sinv_op,) + PA.war(pzT))
            PA.wrote(pzT, z0T)
            pazT = PA.alloc()
            pw1 = PA.alloc()
            pw1T = PA.alloc()
            pw2 = PA.alloc()
            pw2T = PA.alloc()

            def mm4(bank, lhs_store, rhs_store, deps):
                gg = None
                for mi in range(2):
                    for kc in range(2):
                        gg = k.mm1(bank, lhs_store[:, kc * 256 + mi * 128:kc * 256 + (mi + 1) * 128],
                                   rhs_store[:, kc * 256:(kc + 1) * 256],
                                   start=(mi == 0 and kc == 0), stop=(mi == 1 and kc == 1),
                                   deps=deps,
                                   out_ap=k.pb[bank][:, mi * 256:(mi + 1) * 256])
                return gg

            zdep = (z0, z0T)
            for it in range(ATA_NS):
                g_az = mm4(5, aT_ap(h), PA.ap(pz), zdep + tuple(a_rd[h]))
                a_rd[h].append(g_az)
                if ATA_NSOP <= 1:
                    zdep = (g_az, z0T)
                    break
                g_azT = mm4(6, PA.ap(pz), aT_ap(h), zdep)
                PA.read(pz, g_azT)
                a_rd[h].append(g_azT)
                w1 = k.rd(5, "vector", lambda e, pw1=pw1: e.tensor_sub(
                    PA.ap(pw1), PA.ap(iI["i7_bf"]), k.pb[5][:, 0:512]), deps=PA.war(pw1))
                PA.wrote(pw1, w1)
                if ATA_NSOP <= 2:
                    zdep = (w1, g_azT)
                    break
                if ATA_NSOP == 21:
                    azT = k.rd(6, "scalar", lambda e, pazT=pazT: e.copy(
                        out=PA.ap(pazT), in_=k.pb[6][:, 0:512]), deps=PA.war(pazT))
                    PA.wrote(pazT, azT)
                    zdep = (w1, azT)
                    break
                if ATA_NSOP == 22:
                    w1T = k.rd(6, "vector", lambda e, pw1T=pw1T: e.tensor_sub(
                        PA.ap(pw1T), PA.ap(iI["i7_bf"]), k.pb[6][:, 0:512]), deps=PA.war(pw1T))
                    PA.wrote(pw1T, w1T)
                    zdep = (w1, w1T)
                    break
                azT = k.rd(6, "scalar", lambda e, pazT=pazT: e.copy(
                    out=PA.ap(pazT), in_=k.pb[6][:, 0:512]), deps=PA.war(pazT))
                PA.wrote(pazT, azT)
                w1T = k.rd(6, "vector", lambda e, pw1T=pw1T: e.tensor_sub(
                    PA.ap(pw1T), PA.ap(iI["i7_bf"]), k.pb[6][:, 0:512]),
                    deps=PA.war(pw1T) + (azT,))
                PA.wrote(pw1T, w1T)
                PA.read(iI["i7_bf"], w1)
                PA.read(iI["i7_bf"], w1T)
                if ATA_NSOP <= 3:
                    zdep = (w1, w1T)
                    break
                g_m2 = mm4(5, PA.ap(pazT), PA.ap(pw1), (azT, w1))
                PA.read(pazT, g_m2)
                PA.read(pw1, g_m2)
                if ATA_NSOP <= 4:
                    zdep = (g_m2, w1T)
                    break
                g_m2T = mm4(6, PA.ap(pw1), PA.ap(pazT), (azT, w1T, w1))
                PA.read(pazT, g_m2T)
                PA.read(pw1, g_m2T)
                w2 = k.rd(5, "vector", lambda e, pw2=pw2: e.tensor_sub(
                    PA.ap(pw2), PA.ap(iI["i15_bf"]), k.pb[5][:, 0:512]), deps=PA.war(pw2))
                PA.wrote(pw2, w2)
                if ATA_NSOP <= 5:
                    zdep = (w2, g_m2T)
                    break
                w2T = k.rd(6, "vector", lambda e, pw2T=pw2T: e.tensor_sub(
                    PA.ap(pw2T), PA.ap(iI["i15_bf"]), k.pb[6][:, 0:512]), deps=PA.war(pw2T))
                PA.wrote(pw2T, w2T)
                PA.read(iI["i15_bf"], w2)
                PA.read(iI["i15_bf"], w2T)
                g_m3 = mm4(5, PA.ap(pazT), PA.ap(pw2), (w2,))
                PA.read(pazT, g_m3)
                PA.read(pw2, g_m3)
                if ATA_NSOP <= 6:
                    zdep = (g_m3, w2T)
                    break
                g_m3T = mm4(6, PA.ap(pw2), PA.ap(pazT), (w2, w2T))
                PA.read(pazT, g_m3T)
                PA.read(pw2, g_m3T)
                if ATA_NSOP <= 7:
                    zdep = (g_m3, g_m3T)
                    break
                w3 = k.rd(5, "vector", lambda e, pw1=pw1: e.tensor_sub(
                    PA.ap(pw1), PA.ap(iI["i13_bf"]), k.pb[5][:, 0:512]), deps=PA.war(pw1))
                PA.wrote(pw1, w3)
                w3T = k.rd(6, "vector", lambda e, pw1T=pw1T: e.tensor_sub(
                    PA.ap(pw1T), PA.ap(iI["i13_bf"]), k.pb[6][:, 0:512]), deps=PA.war(pw1T))
                PA.wrote(pw1T, w3T)
                PA.read(iI["i13_bf"], w3)
                PA.read(iI["i13_bf"], w3T)
                if ATA_NSOP <= 8:
                    zdep = (w3, w3T)
                    break
                g_z = mm4(5, PA.ap(pzT), PA.ap(pw1), (w3,))
                PA.read(pzT, g_z)
                PA.read(pw1, g_z)
                if ATA_NSOP <= 9:
                    zdep = (g_z, w3T)
                    break
                g_zT = mm4(6, PA.ap(pw1), PA.ap(pzT), (w3, w3T))
                PA.read(pzT, g_zT)
                PA.read(pw1T, g_zT)
                zn = k.rd(5, "scalar", lambda e, pz=pz: e.activation(
                    out=PA.ap(pz), in_=k.pb[5][:, 0:512], func=AF.Copy, scale=0.25),
                    deps=PA.war(pz))
                PA.wrote(pz, zn)
                znT = k.rd(6, "scalar", lambda e, pzT=pzT: e.activation(
                    out=PA.ap(pzT), in_=k.pb[6][:, 0:512], func=AF.Copy, scale=0.25),
                    deps=PA.war(pzT))
                PA.wrote(pzT, znT)
                zdep = (zn, znT)
            zslots[hh] = (pzT, zdep[1])
            PA.release(pz)
            for p_ in (pazT, pw1, pw1T, pw2, pw2T):
                PA.release(p_)
        if ATA_P2 == "ns":
            for hh2 in range(2):
                PA.release(zslots[hh2][0])
            qk_rd[0] = qd[-1:]
            qk_rd[1] = kd[-1:]
            vp_rd = vdone[-1:]
            continue
        for hh in range(2):
            h = 2 * hp + hh
            r0 = 64 * hh
            pzT, zT_done = zslots[hh]
            at1_rd = []
            for mi in range(2):
                for nt in range(4):
                    sl = slice(nt * 512, (nt + 1) * 512)
                    bank = nt % 2
                    g1 = k.mm1(bank, klT_ap(hp)[r0:r0 + 64, mi * 128:(mi + 1) * 128],
                               ARENA[0].ap()[r0:r0 + 64, sl], start=True, stop=True,
                               deps=(qd[-1], sinv_op))
                    qk_rd[0].append(g1)
                    e1 = k.rd(bank, "scalar", lambda e, mi=mi, sl=sl, bank=bank, hh=hh: e.activation(
                        out=ARENA[2 + mi].ap()[:, sl], in_=k.pb[bank][:, 0:512],
                        func=AF.Exp, bias=bbias[:, 0 + hh:1 + hh]),
                        deps=(bM1[hh],) + tuple(a23_rd))
                    bias_rd.append(e1)
                    a23_rd = []
                    at1_rd.append(e1)
            if ATA_P2 == "sims1":
                PA.release(pzT)
                continue
            at3_rd = []
            for kb in range(16):
                bank = 2 + kb % 2
                g3 = k.mm1(bank, ARENA[1].ap()[r0:r0 + 64, kb * 128:(kb + 1) * 128],
                           qlT.ap()[r0:r0 + 64, hp * 256:hp * 256 + 256], start=True, stop=True,
                           deps=(kd[-1], sinv_op))
                qk_rd[1].append(g3)
                ql_rd.append(g3)
                e3 = k.rd(bank, "scalar", lambda e, kb=kb, bank=bank, hh=hh: e.activation(
                    out=ARENA[4 + kb // 8].ap()[:, (kb % 8) * 256:(kb % 8) * 256 + 256],
                    in_=k.pb[bank][:, 0:256], func=AF.Exp, bias=bbias[:, 2 + hh:3 + hh]),
                    deps=(bM3[hh],) + tuple(a45_rd))
                bias_rd.append(e3)
                a45_rd = []
                at3_rd.append(e3)
            if ATA_P2 == "sims3":
                PA.release(pzT)
                continue
            gr = None
            first = True
            for mi in range(2):
                for kb in range(16):
                    lhs = ARENA[4 + kb // 8].ap()[:, (kb % 8) * 256 + mi * 128:(kb % 8) * 256 + (mi + 1) * 128]
                    gr = k.mm1(7, lhs, vpack.ap()[:, kb * 128 + r0:kb * 128 + r0 + 64],
                               start=first, stop=False, deps=(at3_rd[kb], vdone[kb]),
                               out_ap=k.pb[7][:, mi * 65:mi * 65 + 64])
                    first = False
                    gr = k.mm1(7, lhs, consts["ones_col_bf"][:],
                               start=False, stop=(mi == 1 and kb == 15),
                               deps=(cdeps["ones_col_bf"],),
                               out_ap=k.pb[7][:, mi * 65 + 64:mi * 65 + 65])
                    a45_rd.append(gr)
            pr3 = PA.alloc()
            r3ev = []
            for mi in range(2):
                rc = k.rd(7, "vector", lambda e, mi=mi: e.reciprocal(
                    SMW[:, 28:29], k.pb[7][:, mi * 65 + 64:mi * 65 + 65]),
                    deps=tuple(smw_rd))
                smw_rd = []
                ev = k.rd(7, "scalar", lambda e, mi=mi, pr3=pr3: e.activation(
                    out=PA.ap(pr3)[:, mi * 64:(mi + 1) * 64], in_=k.pb[7][:, mi * 65:mi * 65 + 64],
                    func=AF.Copy, scale=SMW[:, 28:29]),
                    deps=(rc,) + (PA.war(pr3) if mi == 0 else ()))
                smw_rd = [ev]
                r3ev.append(ev)
            PA.wrote(pr3, r3ev[-1])
            if ATA_P2 == "r3":
                PA.release(pr3)
                PA.release(pzT)
                continue
            pza = PA.alloc()
            zev = []
            first = True
            for mi in range(2):
                gz = None
                for kc in range(2):
                    gz = k.mm1(7, PA.ap(pzT)[:, kc * 256 + mi * 128:kc * 256 + (mi + 1) * 128],
                               PA.ap(pr3)[:, kc * 64:(kc + 1) * 64],
                               start=first, stop=(mi == 1 and kc == 1),
                               deps=(zT_done, r3ev[-1]),
                               out_ap=k.pb[7][:, mi * 65:mi * 65 + 64])
                    first = False
                    PA.read(pzT, gz)
                    PA.read(pr3, gz)
                ze = k.rd(7, "scalar", lambda e, mi=mi, pza=pza: e.copy(
                    out=PA.ap(pza)[:, mi * 65:mi * 65 + 64], in_=k.pb[7][:, mi * 65:mi * 65 + 64]),
                    deps=(PA.war(pza) if mi == 0 else ()))
                zev.append(ze)
            def _mz(e, pza=pza):
                e.memset(PA.ap(pza)[:, 64:65], 1.0)
                return e.memset(PA.ap(pza)[:, 129:130], 1.0)
            mzo = tp.op("vector", _mz, deps=PA.war(pza) + (zev[0],))
            PA.wrote(pza, mzo)
            for ze in zev:
                PA.read(pza, ze)
            if ATA_P2 == "z":
                PA.release(pr3)
                PA.release(pza)
                PA.release(pzT)
                continue
            prb = PA.alloc()
            for nt in range(4):
                sl = slice(nt * 512, (nt + 1) * 512)
                bank = nt % 2
                go = None
                for kc in range(2):
                    go = k.mm1(bank, PA.ap(pza)[:, kc * 65:(kc + 1) * 65],
                               ARENA[2 + kc].ap()[:, sl],
                               start=(kc == 0), stop=(kc == 1),
                               deps=(mzo, zev[-1], at1_rd[7]),
                               out_ap=k.pb[bank][0:65, 0:512])
                    PA.read(pza, go)
                    a23_rd.append(go)
                def _rc2(e, bank=bank):
                    with nc.allow_low_precision(reason="softmax 1/sum bf16"):
                        return e.reciprocal(bufs["rrow"][0:1, 0:512], k.pb[bank][64:65, 0:512])
                rc2 = k.rd(bank, "vector", _rc2, deps=tuple(gs1row_rd))
                gs1row_rd = []
                bmm = k.mm(4, [(consts["ones_row_bf"][0:1, 0:64], bufs["rrow"][0:1, 0:512])],
                           deps=(rc2,), out_ap=k.pb[4][0:64, 0:512])
                rbev = k.rd(4, "scalar", lambda e, prb=prb: e.copy(
                    out=PA.ap(prb)[0:64, :], in_=k.pb[4][0:64, 0:512]), deps=PA.war(prb))
                PA.wrote(prb, rbev)
                gs1row_rd = [bmm]
                evf = k.rd(bank, "vector", lambda e, bank=bank, hp=hp, r0=r0, sl=sl, prb=prb: e.tensor_mul(
                    ACT2[hp][r0:r0 + 64, sl], k.pb[bank][0:64, 0:512], PA.ap(prb)[0:64, 0:512]),
                    deps=(rbev,) + tuple(a_rd[2 * hp]) + tuple(a_rd[2 * hp + 1]))
                a_rd[2 * hp] = []
                a_rd[2 * hp + 1] = []
                PA.read(prb, evf)
                fc_rd.append(evf)
            PA.release(pr3)
            PA.release(pza)
            PA.release(prb)
            PA.release(pzT)
        kl_rd = []
        ql_rd = []
        vp_rd = []

    if ATA_STOP == "pass2":
        return list(x_deps) + list(fc_rd) + list(qk_rd[0]) + list(qk_rd[1])
    taps = SMW
    wc2 = GS[1]
    conv_done = []
    x_new = list(x_deps)
    combined_ops = {}
    for ccx in range(NCC):
        dwc = tp.dma("gpsimd", lambda e, ccx=ccx: e.dma_start(
            out=wc2[0:2, 0:13],
            in_=ins["L0_attn_a_w_conv"].rearrange("h a kk b -> (a b) h kk")[0, 2 * ccx:2 * ccx + 2, :]),
            deps=tuple(conv_done[-1:]) + tuple(fc_rd[-4:]))
        gt = k.mm(4, [(consts["headselT_f32"][:], wc2[0:2, 0:13])],
                  deps=(dwc, cdeps["headselT_f32"]), ncols=13)
        tev = k.rd(4, "vector", lambda e: e.tensor_copy(taps[:, 0:13], k.pb[4][:, 0:13]),
                   deps=(cdeps.get("_lnp"),))
        tadd = tp.op("vector", lambda e: e.tensor_scalar_add(taps[:, 6:7], taps[:, 6:7], 1.0),
                     force=(tev,))
        c0 = tp.op("vector", lambda e, ccx=ccx: e.tensor_single_scalar(
            ARENA[ccx][:], ACT2[ccx][:], taps[:, 6:7], ALU.mult),
            deps=(tadd,) + tuple(fc_rd) + tuple(qk_rd[0]) + tuple(qk_rd[1]) + tuple(vp_rd))
        prev = c0
        scr_rd = {0: tuple(conv_done[-1:]), 1: ()}
        js = [j for j in range(13) if j != 6]
        for ji, j in enumerate(js):
            dshift = j - 6
            slo = max(0, dshift)
            shi = 2048 + min(0, dshift)
            dlo = max(0, -dshift)
            sb_ = ji % 2
            scratch = GB[0] if sb_ == 0 else ACT1[ccx]
            ts_o = tp.op("gpsimd", lambda e, ccx=ccx, j=j, slo=slo, shi=shi, scratch=scratch: e.tensor_single_scalar(
                scratch[:, 0:shi - slo], ACT2[ccx][:, slo:shi], taps[:, j:j + 1], ALU.mult),
                deps=(tadd,) + scr_rd[sb_])
            ad_o = tp.op("vector", lambda e, ccx=ccx, dlo=dlo, slo=slo, shi=shi, scratch=scratch: e.tensor_add(
                ARENA[ccx][:, dlo:dlo + shi - slo], ARENA[ccx][:, dlo:dlo + shi - slo],
                scratch[:, 0:shi - slo]), deps=(ts_o, prev), force=(c0,) if ji == 0 else ())
            scr_rd[sb_] = (ad_o,)
            prev = ad_o
        conv_done.append(prev)
        combined_ops[ccx] = prev

    if ATA_STOP == "conv":
        return list(x_deps) + list(conv_done)
    wfc_done = []
    def mk_fc(mi, nt, bank):
        sl = slice(nt * 512, (nt + 1) * 512)
        o = k.rd(bank, "vector", lambda e, mi=mi, sl=sl, bank=bank: e.scalar_tensor_tensor(
            X[mi][:, sl], X[mi][:, sl], sm["bfc"][:, mi:mi + 1], k.pb[bank][:, 0:512],
            ALU.add, ALU.add), deps=(x_new[mi], cdeps.get("_lnp")))
        x_new[mi] = o
        wfc_done.append(o)
    groups = emit_ccpass(k, slab, ins["L0_attn_a_w_fc"], [ARENA[i] for i in range(NCC)],
                         list(combined_ops.values()), mk_fc)
    bufs["arena_war"] = tuple(groups[-8:])
    bufs["act2_war"] = tuple(combined_ops.values())
    bufs["gb_war"] = tuple(conv_done)
    bufs["act1_war"] = tuple(qk_rd[0]) + tuple(qk_rd[1]) + tuple(conv_done)
    scr["ln_prev_users"] = tuple(scr.get("ln_prev_users", ())) + tuple(
        v for rd in PA.readers.values() for v in rd[-1:])
    return x_new


def emit_attn_b(k, consts, cdeps, scr, ins, sm, X, x_deps, bufs, tag="atb"):
    tp = k.tp
    ARENA, ACT1, ACT2, GB = bufs["ARENA"], bufs["ACT1"], bufs["ACT2"], bufs["GB"]
    SMW, SMWB, bbias = bufs["SMW"], bufs["SMWB"], bufs["bbias"]
    slab = bufs["slab"]
    arena_war = list(bufs.get("arena_war", ()))

    xb_ops = emit_ln(k, consts, cdeps, scr, X, x_deps, ACT1, tag=f"{tag}_ln",
                     out_war=bufs.get("act1_war", ()))

    ropeC = [ACT2[4 + nt].ap()[:, 0:512] for nt in range(4)]
    ropeS = [ACT2[4 + nt].ap()[:, 512:1024] for nt in range(4)]
    rope_deps = []
    a2war = tuple(bufs.get("act2_war", ()))
    for nt in range(4):
        rope_deps.append(tp.dma("gpsimd", lambda e, nt=nt: e.dma_start(
            out=ropeC[nt], in_=k.cdram["ropeC_bf"].ap()[:, nt * 512:(nt + 1) * 512]), deps=a2war))
        rope_deps.append(tp.dma("gpsimd", lambda e, nt=nt: e.dma_start(
            out=ropeS[nt], in_=k.cdram["ropeS_bf"].ap()[:, nt * 512:(nt + 1) * 512]), deps=a2war))

    wt = ACT2[2]
    scr3 = ACT2[3]
    rrow = bufs["rrow"]
    vpack = [ARENA[4 + i] for i in range(4)]
    PT = GB[0]
    slot_rd = {i: list(arena_war) for i in range(4)}
    vpack_rd = list(arena_war)
    wt_rd = list(a2war)
    scr3_rd = list(a2war)
    pt_rd = list(bufs.get("gb_war", ()))
    bias_rd = []
    x_new = list(x_deps)

    def wt_block(m, ccx):
        off = m * 1024 + (ccx // 4) * 512 + (ccx % 4) * 128
        return wt[:, off:off + 128]

    for Q in range(ATB_QUARTERS):
        if Q > 0 and ATB_BARRIER:
            tp.ambient = ()
            tp.ambient = tuple(tp.last_per_engine())
        def wtrans_qkv(wdram):
            ops = []
            for m in range(2):
                wbf, cst, bidx = slab.load(
                    lambda m=m, wdram=wdram, Q=Q: wdram[256 * Q + m * 128:256 * Q + (m + 1) * 128, :])
                for g4 in range(2):
                    tb = 6 + g4
                    for j in range(4):
                        ccx = g4 * 4 + j
                        trj = k.transpose_into(tb, j * 128, wbf[:, ccx * 128:(ccx + 1) * 128],
                                               consts["ident_bf"][:],
                                               deps=(cst, cdeps["ident_bf"]), first=(j == 0))
                        slab.used(bidx, trj)
                    ev = k.rd(tb, "scalar", lambda e, m=m, g4=g4, tb=tb: e.copy(
                        out=wt[:, m * 1024 + g4 * 512:m * 1024 + (g4 + 1) * 512],
                        in_=k.pb[tb].ap().bitcast(BF16)[:, 0:512]), deps=tuple(wt_rd))
                    ops.append(ev)
            return ops

        qk_ready = {}
        for wi, wname in enumerate(("L0_attn_b_Wq", "L0_attn_b_Wk")):
            wops = wtrans_qkv(ins[wname])
            wt_rd = []
            for m in range(2):
                slot = ARENA[wi * 2 + m]
                for nt in range(4):
                    sl = slice(nt * 512, (nt + 1) * 512)
                    bank = nt % 2
                    calls = [(wt_block(m, ccx), ACT1[ccx][:, sl]) for ccx in range(NCC)]
                    gmm = k.mm(bank, calls, deps=tuple(wops) + tuple(xb_ops))
                    wt_rd.append(gmm)
                    evo = k.rd(bank, "scalar", lambda e, slot=slot, sl=sl, bank=bank, wi=wi: e.activation(
                        out=slot[:, sl], in_=k.pb[bank][:, 0:512], func=AF.Copy,
                        scale=(0.125 if wi == 0 else 1.0)),
                        deps=tuple(slot_rd[wi * 2 + m]))
                    slot_rd[wi * 2 + m] = []
                    pb2 = 2 + nt % 2
                    pmm = k.mm(pb2, [(consts["perm_bf"][:], slot[:, sl])],
                               deps=(evo, cdeps["perm_bf"]))
                    t1o = tp.op("vector", lambda e, slot=slot, sl=sl, nt=nt: e.tensor_mul(
                        scr3[:, 1536:2048], slot[:, sl], ropeC[nt]),
                        deps=(evo, rope_deps[nt * 2]) + tuple(scr3_rd))
                    scr3_rd = []
                    t2o = k.rd(pb2, "vector", lambda e, sl=sl, nt=nt, pb2=pb2: e.tensor_mul(
                        PT[:, 0:512], k.pb[pb2][:, 0:512], ropeS[nt]),
                        deps=(rope_deps[nt * 2 + 1],) + tuple(pt_rd))
                    pt_rd = []
                    t3o = tp.op("vector", lambda e, slot=slot, sl=sl: e.tensor_add(
                        slot[:, sl], scr3[:, 1536:2048], PT[:, 0:512]), force=(t1o, t2o))
                    scr3_rd.append(t3o)
                    pt_rd.append(t3o)
                    slot_rd[wi * 2 + m] = [t3o]
            qk_ready[wi * 2] = slot_rd[wi * 2][-1]
            qk_ready[wi * 2 + 1] = slot_rd[wi * 2 + 1][-1]

        vops = wtrans_qkv(ins["L0_attn_b_Wv"])
        wt_rd = []
        vdone = []
        for tc in range(16):
            vp = vpack[tc // 4]
            voff = (tc % 4) * 260
            bank = 4
            for m in range(2):
                cl = [(ACT1[ccx][:, tc * 128:(tc + 1) * 128], wt_block(m, ccx)) for ccx in range(NCC)]
                gmm = k.mm(bank, cl,
                           deps=tuple(vops) + tuple(xb_ops),
                           out_ap=k.pb[bank][:, m * 128:(m + 1) * 128])
                wt_rd.append(gmm)
            ev = k.rd(bank, "scalar", lambda e, vp=vp, voff=voff, bank=bank: e.copy(
                out=vp.ap()[:, voff:voff + 260].rearrange("p (h d) -> p h d", h=4)[:, :, 0:64],
                in_=k.pb[bank][:, 0:256].rearrange("p (h d) -> p h d", h=4)),
                deps=tuple(vpack_rd))
            vdone.append(ev)
        vpack_rd = []
        for i in range(4):
            vdone.append(tp.op("vector", lambda e, i=i: e.memset(
                vpack[i].ap()[:, 0:1040].rearrange("p (c h d) -> p c h d", c=4, h=4)[:, :, :, 64:65], 1.0),
                deps=tuple(arena_war)))
        arena_war = []

        woops = []
        for ccx in range(NCC):
            wbf, cst, bidx = slab.load(lambda ccx=ccx: ins["L0_attn_b_Wo"][ccx * 128:(ccx + 1) * 128, :])
            tb = 6 + ccx % 2
            for m in range(2):
                trj = k.transpose_into(tb, m * 128, wbf[:, 256 * Q + m * 128:256 * Q + (m + 1) * 128],
                                       consts["ident_bf"][:], deps=(cst, cdeps["ident_bf"]),
                                       first=(m == 0))
                slab.used(bidx, trj)
            ev = k.rd(tb, "scalar", lambda e, ccx=ccx, tb=tb: e.copy(
                out=wt[:, ccx * 256:(ccx + 1) * 256],
                in_=k.pb[tb].ap().bitcast(BF16)[:, 0:256]), deps=tuple(wt_rd))
            woops.append(ev)
        wt_rd = []

        def woT_block(m2, mi):
            return wt[:, mi * 256 + m2 * 128:mi * 256 + (m2 + 1) * 128]

        bias_ops = [None] * 4
        for m in range(2):
            accq, acck, tmpc = SMW[:, 24:25], SMW[:, 25:26], SMW[:, 26:27]
            accs = []
            for wi in range(2):
                acc = accq if wi == 0 else acck
                slot = ARENA[wi * 2 + m]
                seq = None
                for nt in range(4):
                    sq_o = tp.op("scalar", lambda e, slot=slot, nt=nt: e.activation(
                        out=scr3[:, 0:512], in_=slot[:, nt * 512:(nt + 1) * 512], func=AF.Square),
                        deps=(qk_ready[wi * 2 + m],) + ((seq,) if seq else ()) + tuple(scr3_rd))
                    scr3_rd = []
                    gmm = k.mm(5, [(consts["headsel_bf"][:], scr3.ap()[:, 0:512])],
                               deps=(sq_o, cdeps["headsel_bf"]), out_ap=k.pb[5][0:2, 0:512])
                    slot_rd[wi * 2 + m].append(sq_o)
                    dst = acc if nt == 0 else tmpc
                    red = k.rd(5, "vector", lambda e, dst=dst: e.reduce_max(
                        dst[0:2, :], k.pb[5][0:2, 0:512], AX.X))
                    if nt > 0:
                        red = tp.op("vector", lambda e, acc=acc: e.tensor_max(
                            acc[0:2, :], acc[0:2, :], tmpc[0:2, :]), force=(red,))
                    seq = red
                    scr3_rd.append(gmm)
                accs.append(seq)
            cb = k.vseq("vector", [
                lambda e: e.tensor_add(SMW[:, 26:27][0:2, :], SMW[:, 24:25][0:2, :], SMW[:, 25:26][0:2, :]),
                lambda e: e.tensor_scalar_mul(SMW[:, 26:27][0:2, :], SMW[:, 26:27][0:2, :], -0.5),
                lambda e: e.tensor_copy(SMWB[0:2, 0:1], SMW[:, 26:27][0:2, :]),
            ], deps=tuple(accs))[-1]
            for hh in range(2):
                gmm = k.mm(5, [(consts["rowsel_bf"][:, hh * 128:(hh + 1) * 128], SMWB[0:2, 0:1])],
                           deps=(cb, cdeps["rowsel_bf"]) + tuple(bias_rd), ncols=1)
                bias_rd = []
                bias_ops[m * 2 + hh] = k.rd(5, "vector", lambda e, m=m, hh=hh: e.tensor_copy(
                    bbias[:, m * 2 + hh:m * 2 + hh + 1], k.pb[5][:, 0:1]))

        ot_evacs = []
        prev_norm = {}
        pt_rd_tile = {nt: list(pt_rd) for nt in range(4)}
        pt_rd = []
        for hi in range(4):
            m = hi // 2
            r0 = 64 * (hi % 2)
            qs = ARENA[0 + m]
            ks = ARENA[2 + m]
            pv_group = [None] * 4
            for kb in range(16):
                nt0 = kb // 4
                for nt in range(nt0, 4):
                    sb = 4 + nt % 2
                    colskip = kb * 128 - 512 * nt if nt == nt0 else 0
                    smm = k.mm1(sb, ks[r0:r0 + 64, kb * 128:(kb + 1) * 128],
                                qs[r0:r0 + 64, nt * 512:(nt + 1) * 512],
                                start=True, stop=True,
                                deps=(qk_ready[0 + m], qk_ready[2 + m]))
                    slot_rd[0 + m].append(smm)
                    slot_rd[2 + m].append(smm)
                    exo = k.rd(sb, "scalar", lambda e, sb=sb, nt=nt, cs=colskip, hi=hi, m=m: e.activation(
                        out=PT[:, nt * 512 + cs:(nt + 1) * 512],
                        in_=k.pb[sb][:, cs:512],
                        func=AF.Exp, bias=bbias[:, m * 2 + (hi % 2):m * 2 + (hi % 2) + 1]),
                        deps=(bias_ops[hi],) + tuple(pt_rd_tile[nt]))
                    pt_rd_tile[nt] = []
                    bias_rd.append(exo)
                    dep_pv = exo
                    if nt == nt0:
                        dep_pv = tp.op("vector", lambda e, kb=kb: e.tensor_mul(
                            PT[:, kb * 128:(kb + 1) * 128], PT[:, kb * 128:(kb + 1) * 128],
                            consts["utri_bf"][:]), deps=(exo, cdeps["utri_bf"]))
                    vp = vpack[kb // 4]
                    vb0 = (kb % 4) * 260 + hi * 65
                    pv = k.mm1(nt, vp.ap()[:, vb0:vb0 + 65],
                               PT[:, nt * 512 + colskip:(nt + 1) * 512],
                               start=(kb == 0), stop=(kb == 4 * nt + 3),
                               deps=(dep_pv,) + tuple(vdone) + tuple(prev_norm.get(nt, ())),
                               out_ap=k.pb[nt][0:65, colskip:512])
                    pt_rd_tile[nt].append(pv)
                    pv_group[nt] = pv
            for nt in range(4):
                bank = nt
                def _rc(e, bank=bank):
                    with k.nc.allow_low_precision(reason="softmax 1/sum stored bf16 (matmul operand)"):
                        return e.reciprocal(rrow[0:1, 0:512], k.pb[bank][64:65, 0:512])
                rc = k.rd(bank, "vector", _rc, deps=tuple(prev_norm.get("rrow", ())))
                bmm = k.mm(5, [(consts["ones_row_bf"][0:1, 0:64], rrow[0:1, 0:512])], deps=(rc,),
                           out_ap=k.pb[5][0:64, 0:512])
                rbev = k.rd(5, "scalar", lambda e: e.copy(out=scr3[0:64, 0:512], in_=k.pb[5][0:64, 0:512]),
                            deps=tuple(scr3_rd))
                scr3_rd = []
                dsts = ACT2[m]
                evf = k.rd(bank, "vector", lambda e, bank=bank, dsts=dsts, r0=r0, nt=nt: e.tensor_mul(
                    dsts[r0:r0 + 64, nt * 512:(nt + 1) * 512], k.pb[bank][0:64, 0:512], scr3[0:64, 0:512]),
                    deps=(rbev,) + tuple(bufs.get("act2_war01", ()) if (hi == 0) else ()))
                scr3_rd.append(evf)
                prev_norm[nt] = (evf,)
                prev_norm["rrow"] = (bmm,)
                ot_evacs.append(evf)
        for nt in range(4):
            pt_rd.extend(pt_rd_tile[nt])

        for mi in range(NCC):
            for nt in range(4):
                sl = slice(nt * 512, (nt + 1) * 512)
                bank = 4 + (mi * 4 + nt) % 2
                calls = [(woT_block(m2, mi), ACT2[m2][:, sl]) for m2 in range(2)]
                gmm = k.mm(bank, calls, deps=tuple(woops) + tuple(ot_evacs))
                wt_rd.append(gmm)
                ad = k.rd(bank, "vector", lambda e, mi=mi, sl=sl, bank=bank: e.tensor_add(
                    X[mi][:, sl], X[mi][:, sl], k.pb[bank][:, 0:512]), deps=(x_new[mi],))
                x_new[mi] = ad
        bufs["act2_war01"] = tuple(wt_rd[-8:])
        arena_war = []

    bufs["arena_war"] = tuple(v for sl_ in slot_rd.values() for v in sl_[-4:]) + tuple(vdone[-2:])
    bufs["act2_war"] = tuple(wt_rd[-8:]) + tuple(scr3_rd)
    bufs["gb_war"] = tuple(pt_rd)
    bufs["act1_war"] = tuple(wt_rd[-16:])
    tp.ambient = ()
    return x_new


# version 06 (final)
# speedup vs baseline: 1.2532x; 1.2532x over previous
import contextlib
import numpy as np
import ml_dtypes
import concourse.bass as bass
import concourse.mybir as mybir
from concourse.bass_utils import run_bass_kernel_spmd

F32 = mybir.dt.float32
BF16 = mybir.dt.bfloat16
U8 = mybir.dt.uint8
I32 = mybir.dt.int32
AF = mybir.ActivationFunctionType
ALU = mybir.AluOpType
AX = mybir.AxisListType
BF = ml_dtypes.bfloat16

T = 2048
C = 1024
NCC = 8
H = 16
DH = 64
M_LM = 256
NS_ITERS = 6
HID = 4096
NHC = 32
EPS = 1e-5
N_CORES = 8
QMAGIC = 0x5F3759DF

ENGINES = ("sync", "scalar", "tensor", "vector", "gpsimd")
FFN_STOP = None
ATB_QUARTERS = 4
ATB_BARRIER = False
ATA_LOCAL = False
ATA_STOP = None
ATA_P2 = "full"
ATA_NS = NS_ITERS
ATA_NSOP = 99
ATA_P1 = NCC
RACE_DETECT = False
DUMPS = ()


def _dt(x):
    return x.ap().dtype if hasattr(x, "ap") else x.dtype


class Tape:
    def __init__(self):
        self.ops = []

    ambient = ()

    def op(self, engine, fn, deps=(), amount=1, force=(), stream=None):
        deps = tuple(d for d in deps if d is not None) + tuple(self.ambient)
        force = tuple(d for d in force if d is not None)
        self.ops.append(dict(engine=engine, fn=fn, deps=deps, force=force,
                             amount=amount, needed=False, cum=None, stream=stream))
        return len(self.ops) - 1

    def dma(self, engine, fn, deps=(), stream=None):
        if engine == "sync":
            assert stream is not None, "sync DMAs must declare a stream"
        return self.op(engine, fn, deps, amount=16, stream=stream)

    def last_per_engine(self):
        out = {}
        for i, o in enumerate(self.ops):
            out[o["engine"]] = i
        return tuple(out.values())

    SELF_SYNC = ("vector", "scalar", "gpsimd")

    def emit(self, nc, block, es):
        ops = self.ops
        for o in ops:
            if o["amount"] != 1:
                o["needed"] = True
            for d in o["deps"]:
                if (ops[d]["engine"] != o["engine"] or ops[d]["amount"] != 1
                        or o["engine"] in self.SELF_SYNC):
                    ops[d]["needed"] = True
            for d in o["force"]:
                ops[d]["needed"] = True
        def ekey(o):
            if o["amount"] == 16:
                if o["engine"] == "gpsimd":
                    return "gpsimd_dma"
                return "dma_" + o["stream"]
            return o["engine"]
        keys = list(ENGINES) + ["gpsimd_dma"]
        for o in ops:
            if ekey(o) not in keys:
                keys.append(ekey(o))
        sems = {e: es.enter_context(nc.semaphore(f"sem_{e}")) for e in keys}
        counters = {e: 0 for e in keys}
        for o in ops:
            if o["needed"]:
                counters[ekey(o)] += o["amount"]
                o["cum"] = counters[ekey(o)]

        def section(eng_name):
            def _(eng):
                waited = {}
                for i, o in enumerate(ops):
                    if o["engine"] != eng_name:
                        continue
                    need = {}
                    for d in o["deps"] + o["force"]:
                        od = ops[d]
                        if (d in o["deps"] and od["engine"] == eng_name
                                and od["amount"] == 1 and eng_name not in Tape.SELF_SYNC):
                            continue
                        v = od["cum"]
                        assert v is not None
                        if need.get(ekey(od), -1) < v:
                            need[ekey(od)] = v
                    for en2, v in need.items():
                        if waited.get(en2, -1) < v:
                            eng.wait_ge(sems[en2], v)
                            waited[en2] = v
                    inst = o["fn"](eng)
                    if o["needed"]:
                        assert inst is not None, f"needed op {i} on {eng_name} returned None"
                        inst.then_inc(sems[ekey(o)], o["amount"])
                return None
            return _

        for name in ENGINES:
            getattr(block, name)(section(name))


class K:
    def __init__(self, nc, es):
        self.nc = nc
        self.es = es
        self.tp = Tape()
        self.uid = 0
        self.pb = [es.enter_context(nc.psum_tensor(f"pb{i}", [128, 512], F32))
                   for i in range(8)]
        self.pb_readers = [[] for _ in range(8)]
        self.pb_group = [None] * 8
        self.pb_open = [False] * 8

    def name(self, base):
        self.uid += 1
        return f"{base}_{self.uid}"

    def sb(self, base, shape, dtype):
        return self.es.enter_context(self.nc.sbuf_tensor(self.name(base), shape, dtype))

    def mm1(self, bank, lhsT, rhs, start, stop, deps=(), out_ap=None):
        war = tuple(self.pb_readers[bank]) if start else ()
        out = out_ap if out_ap is not None else self.pb[bank][:, 0:rhs.shape[-1]]

        def fn(eng, out=out, lhsT=lhsT, rhs=rhs, start=start, stop=stop):
            return eng.matmul(out, lhsT, rhs, start=start, stop=stop)
        opid = self.tp.op("tensor", fn, deps=war + tuple(d for d in deps if d is not None))
        if start:
            self.pb_readers[bank] = []
        self.pb_group[bank] = opid
        return opid

    def mm(self, bank, calls, deps=(), ncols=512, out_ap=None):
        war = tuple(self.pb_readers[bank])
        out = out_ap if out_ap is not None else self.pb[bank][:, 0:ncols]
        n = len(calls)

        def fn(eng, calls=calls, out=out, n=n):
            inst = None
            for idx, (l, r) in enumerate(calls):
                inst = eng.matmul(out, l, r, start=(idx == 0), stop=(idx == n - 1))
            return inst
        opid = self.tp.op("tensor", fn, deps=war + tuple(d for d in deps if d is not None))
        self.pb_group[bank] = opid
        self.pb_readers[bank] = []
        return opid

    def rd(self, bank, engine, fn, deps=()):
        opid = self.tp.op(engine, fn,
                          deps=(self.pb_group[bank],) + tuple(d for d in deps if d is not None))
        self.pb_readers[bank].append(opid)
        return opid

    def vseq(self, engine, fns, deps=()):
        prev = None
        ids = []
        for i, fn in enumerate(fns):
            o = self.tp.op(engine, fn, deps=(deps if i == 0 else ()),
                           force=((prev,) if prev is not None else ()))
            ids.append(o)
            prev = o
        return ids

    def transpose_into(self, bank, region_off, in_ap, ident, deps=(), first=True):
        war = tuple(self.pb_readers[bank]) if first else ()
        in_part = in_ap.shape[0]
        in_free = in_ap.shape[-1]
        ps = self.pb[bank]
        if in_ap.dtype == BF16:
            out = ps.ap().bitcast(BF16)[0:in_free, region_off:region_off + in_part]
        else:
            out = ps[0:in_free, region_off:region_off + in_part]

        def fn(eng, out=out, in_ap=in_ap, ident=ident):
            return eng.transpose(out, in_ap, ident)
        opid = self.tp.op("tensor", fn, deps=war + tuple(d for d in deps if d is not None))
        if first:
            self.pb_readers[bank] = []
        self.pb_group[bank] = opid
        return opid



def make_const_arrays():
    cc = {}
    cc["ident_bf"] = np.eye(128, dtype=BF)
    cc["ident_f32"] = np.eye(128, dtype=np.float32)
    cc["ones_row_bf"] = np.ones((1, 512), dtype=BF)
    cc["ones_row_f32"] = np.ones((1, 128), dtype=np.float32)
    cc["ones_col_bf"] = np.ones((128, 1), dtype=BF)
    s2 = np.zeros((128, 2), dtype=BF)
    s2[:, 0] = 1.0
    cc["selA_bf"] = s2
    s2b = np.zeros((128, 2), dtype=BF)
    s2b[:, 1] = 1.0
    cc["selB_bf"] = s2b
    o2 = np.zeros((2, 256), dtype=BF)
    o2[0, 0:128] = 1.0
    o2[1, 128:256] = 1.0
    cc["rowsel_bf"] = o2
    i128 = np.eye(128, dtype=np.float32)
    z = np.zeros((128, 128), dtype=np.float32)
    lay = np.concatenate([i128, z, z, i128], axis=1)
    for kv in (7.0, 15.0, 13.0):
        cc[f"i{int(kv)}_bf"] = (lay * kv).astype(BF)
    sel = np.zeros((4, 512), dtype=np.float32)
    for e in range(4):
        sel[e, e * 128:(e + 1) * 128] = 1.0
    cc["sel4_bf"] = sel.astype(BF)
    idx = np.arange(128)
    cc["utri_bf"] = (idx[None, :] >= idx[:, None]).astype(BF)
    half = DH // 2
    inv_freq = (1.0 / 10000.0 ** (np.arange(0, DH, 2, dtype=np.float64) / DH))
    t = np.arange(T, dtype=np.float64)
    fr = np.outer(t, inv_freq)
    cosd = np.cos(fr)
    sind = np.sin(fr)
    Cm = np.zeros((128, T), dtype=np.float64)
    Sm = np.zeros((128, T), dtype=np.float64)
    for d in range(128):
        i = (d % DH) // 2
        Cm[d] = cosd[:, i]
        Sm[d] = sind[:, i] * (-1.0 if d % 2 == 0 else 1.0)
    cc["ropeC_bf"] = Cm.astype(BF)
    cc["ropeS_bf"] = Sm.astype(BF)
    P = np.zeros((128, 128), dtype=np.float32)
    for d in range(128):
        P[d, d ^ 1] = 1.0
    cc["perm_bf"] = P.astype(BF)
    hs = np.zeros((128, 2), dtype=np.float32)
    hs[:64, 0] = 1.0
    hs[64:, 1] = 1.0
    cc["headsel_bf"] = hs.astype(BF)
    cc["headselT_f32"] = np.ascontiguousarray(hs.T.astype(np.float32))
    return cc


DEFER_CONSTS = ("ropeC_bf", "ropeS_bf", "i7_bf", "i15_bf", "i13_bf")


def load_consts(k, carrs):
    out = {}
    deps = {}
    k.cdram = {}
    for name, arr in carrs.items():
        if name in DEFER_CONSTS:
            k.cdram[name] = k.nc.inline_tensor(np.ascontiguousarray(arr), k.name(f"c_{name}"))
            continue
        dt = BF16 if arr.dtype == BF else F32
        dram = k.nc.inline_tensor(np.ascontiguousarray(arr), k.name(f"c_{name}"))
        t = k.sb(f"c_{name}", list(arr.shape), dt)
        d = k.tp.dma("gpsimd", lambda e, t=t, dram=dram: e.dma_start(out=t[:], in_=dram.ap()))
        out[name] = t
        deps[name] = d
    return out, deps



def emit_ln(k, consts, cdeps, scr, src, src_deps, out, gamma=None, beta=None,
            banks=(6, 7), tag="ln", out_war=()):
    tp = k.tp
    s = scr
    cast_b = s["ln_cast"]
    sq_b = s["ln_sq"]
    mini = s["ln_mini"]
    stat, istat = s["ln_stat"], s["ln_istat"]
    ab2 = s["ln_ab2"]
    ast, bst = s["ln_ast"], s["ln_bst"]
    prev_users = s.get("ln_prev_users", ())
    gate_row = s.get("ln_gate_row", ())
    cast_prev = dict(s.get("ln_cast_prev", {}))
    mini_war = s.get("ln_mini_war", ())
    src_is_bf = _dt(src[0]) == BF16

    bounce = []
    for nt in range(4):
        sl = slice(nt * 512, (nt + 1) * 512)
        bank = banks[nt % 2]
        nmm = 0
        for ccx in range(NCC):
            if src_is_bf:
                rhs_ap = src[ccx][:, sl]
                cst = src_deps[ccx]
            else:
                b = ccx % 2
                cst = tp.op("gpsimd", lambda e, ccx=ccx, b=b, sl=sl: e.tensor_copy(
                    cast_b[b][:], src[ccx][:, sl]),
                    deps=(src_deps[ccx], cast_prev.get(("c", b))) + tuple(prev_users))
                rhs_ap = cast_b[b][:]
            m1 = k.mm1(bank, consts["selA_bf"][:], rhs_ap, start=(nmm == 0), stop=False,
                       deps=(cst, cdeps["selA_bf"]), out_ap=k.pb[bank][0:2, 0:512])
            if not src_is_bf:
                cast_prev[("c", ccx % 2)] = m1
            nmm += 1
            b2 = 0
            cst2 = tp.op("scalar", lambda e, ccx=ccx, b2=b2, sl=sl: e.activation(
                out=sq_b[b2][:], in_=src[ccx][:, sl], func=AF.Square),
                deps=(src_deps[ccx], cast_prev.get(("s", b2))) + tuple(prev_users))
            m2 = k.mm1(bank, consts["selB_bf"][:], sq_b[b2][:],
                       start=False, stop=(ccx == NCC - 1),
                       deps=(cst2, cdeps["selB_bf"]), out_ap=k.pb[bank][0:2, 0:512])
            cast_prev[("s", b2)] = m2
            nmm += 1
        evd = k.rd(bank, "scalar", lambda e, bank=bank: e.copy(
            out=mini[0:2, :], in_=k.pb[bank][0:2, 0:512]), deps=tuple(bounce) + tuple(mini_war))
        b1 = tp.dma("gpsimd", lambda e, nt=nt: e.dma_start(
            out=stat[:, nt * 4:(nt + 1) * 4], in_=mini[0:1, :]), deps=(evd,))
        b2d = tp.dma("gpsimd", lambda e, nt=nt: e.dma_start(
            out=stat[:, 16 + nt * 4:16 + (nt + 1) * 4], in_=mini[1:2, :]), deps=(evd,))
        bounce.extend([b1, b2d])

    S = stat
    m = S[:, 32:48]
    var = S[:, 48:64]
    y = S[:, 16:32]
    steps = [
        lambda e: e.tensor_scalar_mul(m, S[:, 0:16], 1.0 / C),
        lambda e: e.tensor_scalar_mul(var, S[:, 16:32], 1.0 / C),
        lambda e: e.tensor_mul(S[:, 0:16], m, m),
        lambda e: e.tensor_sub(var, var, S[:, 0:16]),
        lambda e: e.tensor_scalar_add(var, var, EPS),
        lambda e: e.tensor_single_scalar(istat[:, 0:16], var.bitcast(I32), 1, ALU.arith_shift_right),
        lambda e: e.tensor_scalar(istat[:, 0:16], istat[:, 0:16], -1, QMAGIC, ALU.mult, ALU.add),
        lambda e: e.tensor_copy(y, istat[:, 0:16].bitcast(F32)),
    ]
    for _ in range(3):
        steps += [
            lambda e: e.tensor_mul(S[:, 0:16], y, y),
            lambda e: e.tensor_mul(S[:, 0:16], S[:, 0:16], var),
            lambda e: e.tensor_scalar(S[:, 0:16], S[:, 0:16], -0.5, 1.5, ALU.mult, ALU.add),
            lambda e: e.tensor_mul(y, y, S[:, 0:16]),
        ]
    steps += [
        lambda e: e.tensor_copy(ast[:], y),
        lambda e: e.tensor_mul(S[:, 0:16], m, y),
        lambda e: e.tensor_scalar_mul(bst[:], S[:, 0:16], -1.0),
    ]
    rm = k.vseq("vector", steps, deps=tuple(bounce))[-1]
    rowd = []
    for nt in range(4):
        rowd.append(tp.dma("gpsimd", lambda e, nt=nt: e.dma_start(
            out=ab2[0:1, nt * 512:(nt + 1) * 512], in_=ast[:, nt * 4:(nt + 1) * 4]),
            deps=(rm,) + tuple(gate_row)))
        rowd.append(tp.dma("gpsimd", lambda e, nt=nt: e.dma_start(
            out=ab2[1:2, nt * 512:(nt + 1) * 512], in_=bst[:, nt * 4:(nt + 1) * 4]),
            deps=(rm,) + tuple(gate_row)))

    out_ops = [None] * NCC
    bca_mm = []
    apply_last = {}
    for nt in range(4):
        sl = slice(nt * 512, (nt + 1) * 512)
        gA = k.mm(banks[0], [(consts["rowsel_bf"][:, 0:128], ab2[0:2, sl])],
                  deps=tuple(rowd) + (cdeps["rowsel_bf"],))
        gB = k.mm(banks[1], [(consts["rowsel_bf"][:, 128:256], ab2[0:2, sl])],
                  deps=tuple(rowd))
        bca_mm.extend([gA, gB])
        for ccx in range(NCC):
            o1 = k.rd(banks[0], "vector", lambda e, ccx=ccx, sl=sl: e.tensor_mul(
                out[ccx][:, sl], src[ccx][:, sl], k.pb[banks[0]][:, 0:512]),
                deps=(src_deps[ccx],) + tuple(out_war))
            o2 = k.rd(banks[1], "vector", lambda e, ccx=ccx, sl=sl: e.tensor_add(
                out[ccx][:, sl], out[ccx][:, sl], k.pb[banks[1]][:, 0:512]), deps=())
            k.tp.ops[o2]["force"] = k.tp.ops[o2]["force"] + (o1,)
            apply_last[ccx] = o2
    users = []
    for ccx in range(NCC):
        a1 = apply_last[ccx]
        if gamma is not None:
            a1 = tp.op("vector", lambda e, ccx=ccx: e.tensor_scalar(
                out[ccx][:], out[ccx][:], gamma[ccx], beta[ccx], ALU.mult, ALU.add),
                force=(apply_last[ccx],))
        out_ops[ccx] = a1
        users.append(a1)
    s["ln_prev_users"] = tuple(users) + (rm,)
    s["ln_cast_prev"] = cast_prev
    s["ln_gate_row"] = tuple(bca_mm)
    s["ln_mini_war"] = (bounce[-2], bounce[-1])
    return out_ops


def alloc_ln_scratch(k):
    s = {}
    s["ln_cast"] = [k.sb("lncast", [128, 512], BF16) for _ in range(2)]
    s["ln_sq"] = [k.sb("lnsq", [128, 512], BF16) for _ in range(1)]
    s["ln_mini"] = k.sb("lnmini", [2, 512], F32)
    s["ln_stat"] = k.sb("lnstat", [128, 64], F32)
    s["ln_istat"] = k.sb("lnistat", [128, 16], I32)
    s["ln_ab2"] = k.sb("lnab2", [2, T], BF16)
    s["ln_ast"] = k.sb("lnast", [128, 16], BF16)
    s["ln_bst"] = k.sb("lnbst", [128, 16], BF16)
    return s



class Slab:

    def __init__(self, k, tag):
        self.k = k
        self.stream = tag
        self.raw = [k.sb(f"{tag}_raw", [128, 1024], F32)]
        self.bf = [k.sb(f"{tag}_bf", [128, 1024], BF16) for _ in range(2)]
        self.prev_cast = [None]
        self.readers = [[], []]
        self.i = 0

    def load(self, src_ap_fn, deps=()):
        k = self.k
        b = self.i % 2
        self.i += 1
        ld = k.tp.dma("sync", lambda e: e.dma_start(out=self.raw[0][:], in_=src_ap_fn()),
                      deps=(self.prev_cast[0],) + tuple(deps), stream=self.stream)
        cst = k.tp.op("gpsimd", lambda e, b=b: e.tensor_copy(self.bf[b][:], self.raw[0][:]),
                      deps=(ld,) + tuple(self.readers[b]))
        self.prev_cast[0] = cst
        self.readers[b] = []
        return self.bf[b], cst, b

    def used(self, b, opid):
        self.readers[b].append(opid)


def cc_slab_ap(wdram, mi):
    return wdram.rearrange("(cc p) o -> p cc o", p=128)[:, :, mi * 128:(mi + 1) * 128]


def emit_ccpass(k, slab, wdram, rhs, rhs_deps, consume, extra_mm_deps=(), tag="cc"):
    groups = []
    rot = getattr(k, "_rot", 0)
    for mi in range(NCC):
        wbf, cst, b = slab.load(lambda mi=mi: cc_slab_ap(wdram, mi))
        for nt in range(4):
            sl = slice(nt * 512, (nt + 1) * 512)
            bank = rot % 6
            rot += 1
            calls = [(wbf[:, cc * 128:(cc + 1) * 128], rhs[cc][:, sl]) for cc in range(NCC)]
            g = k.mm(bank, calls, deps=(cst,) + tuple(rhs_deps) + tuple(extra_mm_deps))
            slab.used(b, g)
            groups.append(g)
            consume(mi, nt, bank)
    k._rot = rot
    return groups



def emit_ffn(k, consts, cdeps, scr, W, sm, X, x_deps, bufs, tag="ffn"):
    tp = k.tp
    ARENA, ACT1, ACT2, GB, GS = bufs["ARENA"], bufs["ACT1"], bufs["ACT2"], bufs["GB"], bufs["GS"]
    GROWS, SMW, RL, SGT = bufs["GROWS"], bufs["SMW"], bufs["RL"], bufs["SGT"]
    SMWB = bufs["SMWB"]
    slab = bufs["slab"]
    arena_war = bufs.get("arena_war", ())

    def _stop(stage, deps):
        if FFN_STOP == stage:
            join = tp.op("vector", lambda e: e.engine_nop(), deps=tuple(deps))
            return [join] * NCC
        return None

    xi_ops = emit_ln(k, consts, cdeps, scr, X, x_deps, ARENA, tag=f"{tag}_ln0",
                     out_war=arena_war)
    r = _stop("ln0", xi_ops)
    if r: return r

    g1 = [sm["lnp"][:, 0 + cc:0 + cc + 1] for cc in range(NCC)]
    b1 = [sm["lnp"][:, 8 + cc:8 + cc + 1] for cc in range(NCC)]
    g2 = [sm["lnp"][:, 16 + cc:16 + cc + 1] for cc in range(NCC)]
    b2 = [sm["lnp"][:, 24 + cc:24 + cc + 1] for cc in range(NCC)]
    xln_ops = emit_ln(k, consts, cdeps, scr, ARENA, xi_ops, ACT1, gamma=g1, beta=b1,
                      tag=f"{tag}_ln1")
    r = _stop("ln1", xln_ops)
    if r: return r

    u_ops = {}

    def mk_sig(mi, nt, bank):
        sl = slice(nt * 512, (nt + 1) * 512)
        o1 = k.rd(bank, "scalar", lambda e, mi=mi, sl=sl, bank=bank: e.activation(
            out=ACT2[mi][:, sl], in_=k.pb[bank][:, 0:512], func=AF.Tanh, scale=0.5))
        o2 = tp.op("vector", lambda e, mi=mi, sl=sl: e.tensor_scalar(
            ACT2[mi][:, sl], ACT2[mi][:, sl], 0.5, 0.5, ALU.mult, ALU.add), deps=(o1,))
        u_ops[(mi, nt)] = o2
    wr_groups = emit_ccpass(k, slab, W["Wr"], ACT1, xln_ops, mk_sig)
    r = _stop("u", list(u_ops.values()))
    if r: return r

    gated_ops = {}

    def mk_gate(mi, nt, bank):
        sl = slice(nt * 512, (nt + 1) * 512)
        o = k.rd(bank, "vector", lambda e, mi=mi, sl=sl, bank=bank: e.tensor_mul(
            ACT2[mi][:, sl], ACT2[mi][:, sl], k.pb[bank][:, 0:512]), deps=())
        k.tp.ops[o]["force"] = k.tp.ops[o]["force"] + (u_ops[(mi, nt)],)
        gated_ops[(mi, nt)] = o
    wv_groups = emit_ccpass(k, slab, W["Wv"], ACT1, xln_ops, mk_gate)
    r = _stop("gated", list(gated_ops.values()))
    if r: return r

    att_ops = {}

    def mk_att(mi, nt, bank):
        sl = slice(nt * 512, (nt + 1) * 512)
        o = k.rd(bank, "vector", lambda e, mi=mi, sl=sl, bank=bank: e.tensor_add(
            ARENA[mi][:, sl], ARENA[mi][:, sl], k.pb[bank][:, 0:512]))
        att_ops[(mi, nt)] = o
    wo_groups = emit_ccpass(k, slab, W["Wo"], ACT2, list(gated_ops.values()), mk_att)
    r = _stop("att", list(att_ops.values()))
    if r: return r

    state_ops = {}

    def mk_state(mi, nt, bank):
        sl = slice(nt * 512, (nt + 1) * 512)
        o = k.rd(bank, "scalar", lambda e, mi=mi, sl=sl, bank=bank: e.copy(
            out=ACT2[mi][:, sl], in_=k.pb[bank][:, 0:512]), deps=tuple(wo_groups))
        state_ops[(mi, nt)] = o
    ws_groups = emit_ccpass(k, slab, W["Ws"], ACT1, xln_ops, mk_state)
    r = _stop("state", list(state_ops.values()))
    if r: return r

    xi2_ops = [None] * NCC
    for cc in range(NCC):
        xi2_ops[cc] = att_ops[(cc, 0)]
    xi2_all = list(att_ops.values())

    h_ops = emit_ln(k, consts, cdeps, scr, ARENA, xi2_all, ACT1, gamma=g2, beta=b2,
                    tag=f"{tag}_ln2", out_war=tuple(ws_groups))

    r = _stop("h", h_ops)
    if r: return r
    xadd_ops = []
    for cc in range(NCC):
        o = tp.op("gpsimd", lambda e, cc=cc: e.tensor_add(X[cc][:], X[cc][:], ARENA[cc][:]),
                  deps=tuple(h_ops) + (x_deps[cc],))
        xadd_ops.append(o)

    grow_ops = []
    for tc in range(16):
        tsl = slice(tc * 128, (tc + 1) * 128)
        bank = (tc % 4)
        calls = [(ACT1[cc][:, tsl], sm["packT"][cc][:]) for cc in range(NCC)]
        gmm = k.mm(bank, calls, deps=tuple(h_ops) + (cdeps["_packT"],), ncols=8)
        o1 = k.rd(bank, "scalar", lambda e, bank=bank: e.activation(
            out=SMW[:, 0:4], in_=k.pb[bank][:, 0:4], func=AF.Tanh, scale=0.5),
            deps=(grow_ops[-1],) if grow_ops else ())

        S = SMW
        psl = k.pb[bank]
        wsteps = [
            lambda e: e.tensor_scalar(S[:, 0:4], S[:, 0:4], 0.5, 0.5, ALU.mult, ALU.add),
            lambda e: e.tensor_mul(S[:, 4:7], S[:, 0:3], sm["capb3"][:, 0:3]),
            lambda e: e.tensor_single_scalar(S[:, 4:7], S[:, 4:7], S[:, 3:4], ALU.mult),
            lambda e, psl=psl: e.tensor_add(S[:, 4:7], S[:, 4:7], psl[:, 4:7]),
            lambda e: e.tensor_tensor(S[:, 7:8], S[:, 4:5], S[:, 5:6], ALU.is_ge),
            lambda e: e.tensor_tensor(S[:, 8:9], S[:, 4:5], S[:, 6:7], ALU.is_ge),
            lambda e: e.tensor_mul(S[:, 16:17], S[:, 7:8], S[:, 8:9]),
            lambda e: e.tensor_tensor(S[:, 7:8], S[:, 5:6], S[:, 4:5], ALU.is_gt),
            lambda e: e.tensor_tensor(S[:, 8:9], S[:, 5:6], S[:, 6:7], ALU.is_ge),
            lambda e: e.tensor_mul(S[:, 17:18], S[:, 7:8], S[:, 8:9]),
            lambda e: e.tensor_tensor(S[:, 7:8], S[:, 6:7], S[:, 4:5], ALU.is_gt),
            lambda e: e.tensor_tensor(S[:, 8:9], S[:, 6:7], S[:, 5:6], ALU.is_gt),
            lambda e: e.tensor_mul(S[:, 18:19], S[:, 7:8], S[:, 8:9]),
            lambda e: e.tensor_mul(S[:, 9:10], S[:, 16:17], S[:, 0:1]),
            lambda e: e.tensor_mul(S[:, 10:11], S[:, 17:18], S[:, 1:2]),
            lambda e: e.tensor_add(S[:, 9:10], S[:, 9:10], S[:, 10:11]),
            lambda e: e.tensor_mul(S[:, 10:11], S[:, 18:19], S[:, 2:3]),
            lambda e: e.tensor_add(S[:, 11:12], S[:, 9:10], S[:, 10:11]),
            lambda e: e.tensor_scalar_add(S[:, 12:13], S[:, 11:12], 1e-6),
            lambda e: e.reciprocal(S[:, 13:14], S[:, 12:13]),
            lambda e: e.tensor_mul(S[:, 12:13], S[:, 11:12], S[:, 13:14]),
            lambda e: e.tensor_mul(S[:, 20:21], S[:, 16:17], S[:, 12:13]),
            lambda e: e.tensor_mul(S[:, 21:22], S[:, 17:18], S[:, 12:13]),
            lambda e: e.tensor_mul(S[:, 22:23], S[:, 18:19], S[:, 12:13]),
        ]
        wids = k.vseq("vector", wsteps, deps=(o1, cdeps["_capb3"], k.pb_group[bank]))
        k.pb_readers[bank].append(wids[3])
        o2 = wids[-1]
        o3 = tp.op("vector", lambda e: e.tensor_copy(SMWB[:], SMW[:, 20:24]), force=(o2,))
        tb = 4 + (tc % 2)
        tr = k.transpose_into(tb, 0, SMWB[:], consts["ident_bf"][:], deps=(o3, cdeps["ident_bf"]))
        ev = k.rd(tb, "scalar", lambda e, tsl=tsl, tb=tb: e.copy(
            out=GROWS[0:4, tsl], in_=k.pb[tb].ap().bitcast(BF16)[0:4, 0:128]))
        grow_ops.append(ev)

    r = _stop("grows", grow_ops + xadd_ops)
    if r: return r
    gb_state = {"readers": ()}

    def build_gb(e_i):
        ops_e = []
        for nt in range(4):
            sl = slice(nt * 512, (nt + 1) * 512)
            bank = nt % 2
            g = k.mm(bank, [(consts["sel4_bf"][:, e_i * 128:(e_i + 1) * 128], GROWS[:, sl])],
                     deps=tuple(grow_ops) + (cdeps["sel4_bf"],) + gb_state["readers"])
            o = k.rd(bank, "vector", lambda e, sl=sl, bank=bank: e.tensor_copy(
                GB[0][:, sl], k.pb[bank][:, 0:512]))
            ops_e.append(o)
        gb_state["readers"] = ()
        return ops_e

    g2bf = [ARENA[i].ap() for i in range(NCC)]
    slab2 = bufs["slab2"]
    g2_ops = {}
    sg_prev = {}
    for mi in range(NCC):
        w2bf, c2, b2i = slab2.load(lambda mi=mi: cc_slab_ap(W["W2"], mi))
        w1bf, c1, b1i = slab.load(lambda mi=mi: cc_slab_ap(W["W1"], mi))
        for nt in range(4):
            sl = slice(nt * 512, (nt + 1) * 512)
            bank = (mi * 4 + nt) % 4
            sgb = SGT[nt % 2]
            calls2 = [(w2bf[:, cc * 128:(cc + 1) * 128], ACT2[cc][:, sl]) for cc in range(NCC)]
            gmm2 = k.mm(bank, calls2, deps=(c2,) + tuple(state_ops.values()))
            slab2.used(b2i, gmm2)
            so1 = k.rd(bank, "scalar", lambda e, bank=bank, sgb=sgb: e.activation(
                out=sgb[:], in_=k.pb[bank][:, 0:512], func=AF.Tanh, scale=0.5),
                deps=(sg_prev.get(nt % 2),))
            so2 = tp.op("vector", lambda e, sgb=sgb: e.tensor_scalar(
                sgb[:], sgb[:], 0.5, 0.5, ALU.mult, ALU.add), deps=(so1,))
            bank1 = 4 + (nt % 2)
            calls1 = [(w1bf[:, cc * 128:(cc + 1) * 128], ACT1[cc][:, sl]) for cc in range(NCC)]
            gmm1 = k.mm(bank1, calls1, deps=(c1,) + tuple(h_ops))
            slab.used(b1i, gmm1)
            o = k.rd(bank1, "vector", lambda e, mi=mi, sl=sl, bank1=bank1, sgb=sgb: e.tensor_mul(
                g2bf[mi][:, sl], k.pb[bank1][:, 0:512], sgb[:]), deps=(xadd_ops[mi],))
            k.tp.ops[o]["force"] = k.tp.ops[o]["force"] + (so2,)
            g2_ops[(mi, nt)] = o
            sg_prev[nt % 2] = o

    r = _stop("g2", list(g2_ops.values()) + xadd_ops)
    if r: return r
    gb2_ops = build_gb(2)
    w3_done = []
    w3_gbr = []

    def mk_w3(mi, nt, bank):
        sl = slice(nt * 512, (nt + 1) * 512)
        b = (mi * 4 + nt) % 2
        o1 = k.rd(bank, "vector", lambda e, sl=sl, bank=bank, b=b: e.tensor_mul(
            GS[b][:], k.pb[bank][:, 0:512], GB[0][:, sl]),
            deps=tuple(gb2_ops) + (w3_done[-2] if len(w3_done) >= 2 else None,))
        o2 = tp.op("gpsimd", lambda e, mi=mi, sl=sl, b=b: e.tensor_add(
            X[mi][:, sl], X[mi][:, sl], GS[b][:]), deps=(o1, xadd_ops[mi]))
        w3_done.append(o2)
        w3_gbr.append(o1)
    w3_groups = emit_ccpass(k, slab, W["W3"], g2bf, list(g2_ops.values()), mk_w3)
    gb_state["readers"] = tuple(w3_gbr)

    kslab = bufs["kslab"]
    vslab = bufs["vslab"]
    HRbf = ARENA

    def hr_ap(hc, w):
        i, j = hc // 4, hc % 4
        return HRbf[i][:, j * 512:j * 512 + w]

    r = _stop("w3", w3_done)
    if r: return r
    last_expert_ops = list(w3_done)
    for e_i in range(2):
        gbe_ops = build_gb(e_i)
        gbe_readers = []
        for tt in range(4):
            sl = slice(tt * 512, (tt + 1) * 512)
            hr_ready = []
            for hc in range(NHC):
                kbf, kc, kb = kslab.load(
                    lambda e_i=e_i, hc=hc: W["K_rwkv"][e_i].rearrange("(cc p) h -> p cc h", p=128)[:, :, hc * 128:(hc + 1) * 128])
                bank = hc % 2
                calls = [(kbf[:, cc * 128:(cc + 1) * 128], ACT1[cc][:, sl]) for cc in range(NCC)]
                gmm = k.mm(bank, calls, deps=(kc,) + tuple(h_ops) + tuple(last_expert_ops))
                kslab.used(kb, gmm)
                rb = RL[hc % 2]
                o1 = k.rd(bank, "scalar", lambda e, rb=rb, bank=bank: e.activation(
                    out=rb[:], in_=k.pb[bank][:, 0:512], func=AF.Relu),
                    deps=(hr_ready[hc - 2] if hc >= 2 else None,))
                o2 = tp.op("gpsimd", lambda e, hc=hc, rb=rb: e.tensor_mul(
                    hr_ap(hc, 512), rb[:], rb[:]), deps=(o1,))
                hr_ready.append(o2)
            vgroups = [None] * NCC
            for hc in range(NHC):
                vbf, vc, vb = vslab.load(
                    lambda e_i=e_i, hc=hc: W["V_rwkv"][e_i][hc * 128:(hc + 1) * 128, :])
                for mi in range(NCC):
                    gmm = k.mm1(mi, vbf[:, mi * 128:(mi + 1) * 128], hr_ap(hc, 512),
                                start=(hc == 0), stop=(hc == NHC - 1),
                                deps=(vc, hr_ready[hc]))
                    vslab.used(vb, gmm)
                    vgroups[mi] = gmm
            ev_ops = []
            for mi in range(NCC):
                b = mi % 2
                gbap = GB[0][:, sl]
                o1 = k.rd(mi, "vector", lambda e, mi=mi, b=b, gbap=gbap: e.tensor_mul(
                    GS[b][:], k.pb[mi][:, 0:512], gbap),
                    deps=tuple(gbe_ops) + (ev_ops[-2][1] if len(ev_ops) >= 2 else None,))
                gbe_readers.append(o1)
                o2 = tp.op("gpsimd", lambda e, mi=mi, b=b, sl=sl: e.tensor_add(
                    X[mi][:, sl], X[mi][:, sl], GS[b][:]), deps=(o1,))
                ev_ops.append((o1, o2))
            last_expert_ops = [o2 for (o1, o2) in ev_ops]
        gb_state["readers"] = tuple(gbe_readers)
        r = _stop(f"e{e_i}", last_expert_ops)
        if r: return r

    new_deps = [last_expert_ops[cc] if cc < len(last_expert_ops) else xadd_ops[cc] for cc in range(NCC)]
    bufs["arena_war"] = tuple(last_expert_ops)
    return new_deps



def alloc_bufs(k):
    b = {}
    b["X"] = [k.sb("X", [128, T], F32) for _ in range(NCC)]
    b["ARENA"] = [k.sb("AR", [128, T], BF16) for _ in range(NCC)]
    b["ACT1"] = [k.sb("A1", [128, T], BF16) for _ in range(NCC)]
    b["ACT2"] = [k.sb("A2", [128, T], BF16) for _ in range(NCC)]
    b["GB"] = [k.sb("GB", [128, T], BF16)]
    b["GS"] = [k.sb("GS", [128, 512], F32) for _ in range(2)]
    b["GROWS"] = k.sb("GROWS", [4, T], BF16)
    b["SMW"] = k.sb("SMW", [128, 32], F32)
    b["SMWB"] = k.sb("SMWB", [128, 4], BF16)
    b["RL"] = [k.sb("RL", [128, 512], BF16) for _ in range(2)]
    b["rrow"] = k.sb("rrow", [1, 512], BF16)
    b["bbias"] = k.sb("bbias", [128, 8], F32)

    b["SGT"] = [k.sb("SGT", [128, 512], BF16) for _ in range(2)]
    b["slab"] = Slab(k, "slA")
    b["slab2"] = Slab(k, "slB")
    b["kslab"] = b["slab"]
    b["vslab"] = b["slab2"]
    return b


def load_smalls(k, consts, cdeps, ins, bufs):
    tp = k.tp
    sm = {}
    pstage = bufs["GS"][0].ap()[:, 0:8]
    lnp = k.sb("lnp", [128, 32], F32)
    deps = []
    for i, nm in enumerate(("L0_ffn_ln1_w", "L0_ffn_ln1_b", "L0_ffn_ln2_w", "L0_ffn_ln2_b")):
        for cc in range(NCC):
            d = tp.dma("gpsimd", lambda e, i=i, cc=cc, nm=nm: e.dma_start(
                out=lnp[:, i * 8 + cc:i * 8 + cc + 1],
                in_=ins[nm].rearrange("(cc p) -> cc p", p=128)[cc:cc + 1, :].rearrange("o p -> p o")))
            deps.append(d)
    sm["lnp"] = lnp

    packs = [k.sb("packT", [128, 8], BF16) for _ in range(NCC)]
    pack_deps = []
    prev_cast = None
    for cc in range(NCC):
        csl = slice(cc * 128, (cc + 1) * 128)
        dd = []
        dd.append(tp.dma("gpsimd", lambda e, csl=csl: e.dma_start(
            out=pstage[:, 0:1], in_=ins["L0_ffn_conf_rwkv"][0:1, csl].rearrange("o p -> p o")), deps=(prev_cast,)))
        dd.append(tp.dma("gpsimd", lambda e, csl=csl: e.dma_start(
            out=pstage[:, 1:2], in_=ins["L0_ffn_conf_rwkv"][1:2, csl].rearrange("o p -> p o")), deps=(prev_cast,)))
        dd.append(tp.dma("gpsimd", lambda e, csl=csl: e.dma_start(
            out=pstage[:, 2:3], in_=ins["L0_ffn_conf_trans"].rearrange("(a p) -> a p", a=NCC)[csl.start // 128:csl.start // 128 + 1, :].rearrange("o p -> p o")), deps=(prev_cast,)))
        dd.append(tp.dma("gpsimd", lambda e, csl=csl: e.dma_start(
            out=pstage[:, 3:4], in_=ins["L0_ffn_w_diff"].rearrange("(a p) -> a p", a=NCC)[csl.start // 128:csl.start // 128 + 1, :].rearrange("o p -> p o")), deps=(prev_cast,)))
        dd.append(tp.dma("gpsimd", lambda e, csl=csl: e.dma_start(
            out=pstage[:, 4:7], in_=ins["L0_ffn_W_aff"][csl, :]), deps=(prev_cast,)))
        mz = tp.op("vector", lambda e: e.memset(pstage[:, 7:8], 0.0), deps=(prev_cast,))
        cst = tp.op("gpsimd", lambda e, cc=cc: e.tensor_copy(packs[cc][:], pstage),
                    deps=tuple(dd) + (mz,))
        prev_cast = cst
        pack_deps.append(cst)
    sm["packT"] = packs
    bufs["packs"] = packs
    cap_row = None
    capb3 = k.sb("capb3", [128, 4], F32)
    d = tp.dma("gpsimd", lambda e: e.dma_start(out=capb3[0:1, 0:3], in_=ins["L0_ffn_capital_shares"].rearrange("(o c) -> o c", o=1)))
    g = k.mm(7, [(consts["ones_row_f32"][0:1, 0:128], capb3[0:1, 0:3])],
             deps=(d, cdeps["ones_row_f32"]), ncols=3)
    cb = k.rd(7, "vector", lambda e: e.tensor_copy(capb3[:, 0:3], k.pb[7][:, 0:3]), deps=(d,))
    k.tp.ops[cb]["force"] = k.tp.ops[cb]["force"] + (g,)
    sm["capb3"] = capb3
    bfc = bufs["SMW"].ap()[:, 16:24]
    for cc in range(NCC):
        deps.append(tp.dma("gpsimd", lambda e, cc=cc: e.dma_start(
            out=bfc[:, cc:cc + 1] if hasattr(bfc, "dtype") else bfc[:, cc:cc + 1],
            in_=ins["L0_attn_a_b_fc"].rearrange("(a p) -> a p", a=NCC)[cc:cc + 1, :].rearrange("o p -> p o"))))
    sm["bfc"] = bfc
    cdeps["_lnp"] = deps[-1]
    cdeps["_lnp_all"] = tuple(deps)
    cdeps["_packT"] = pack_deps[-1]
    cdeps["_packT_all"] = tuple(pack_deps)
    cdeps["_capb3"] = cb
    bufs["capb3x"] = [capb3]
    return sm


def emit_load_x(k, consts, cdeps, xin, X, stage):
    tp = k.tp
    deps = [[] for _ in range(NCC)]
    prev_ev = {}
    for tc in range(16):
        b = tc % 2
        war = tuple(prev_ev.get((tc - 2, cc)) for cc in range(NCC)) if tc >= 2 else ()
        d = tp.dma("sync", lambda e, tc=tc, b=b: e.dma_start(
            out=stage[b][:], in_=xin[tc * 128:(tc + 1) * 128, :]), deps=war, stream=f"st{b}")
        for cc in range(NCC):
            bank = cc
            tr = k.transpose_into(bank, 0, stage[b][:, cc * 128:(cc + 1) * 128],
                                  consts["ident_f32"][:], deps=(d, cdeps["ident_f32"]))
            ev = k.rd(bank, "scalar", lambda e, cc=cc, tc=tc, bank=bank: e.copy(
                out=X[cc][:, tc * 128:(tc + 1) * 128], in_=k.pb[bank][:, 0:128]))
            prev_ev[(tc, cc)] = ev
            deps[cc].append(ev)
    lastev = {0: prev_ev[(14, NCC - 1)], 1: prev_ev[(15, NCC - 1)]}
    return [deps[cc][-1] for cc in range(NCC)], lastev


def emit_store_x(k, consts, cdeps, X, x_deps, yout, stage, stage_war=()):
    tp = k.tp
    st_ops = []
    prev_dma = {}
    for tc in range(16):
        b = tc % 2
        evs = []
        for cc in range(NCC):
            bank = cc
            tr = k.transpose_into(bank, 0, X[cc][:, tc * 128:(tc + 1) * 128],
                                  consts["ident_f32"][:],
                                  deps=(x_deps[cc], cdeps["ident_f32"]))
            ev = k.rd(bank, "scalar", lambda e, cc=cc, bank=bank, b=b: e.copy(
                out=stage[b][:, cc * 128:(cc + 1) * 128], in_=k.pb[bank][:, 0:128]),
                deps=(prev_dma.get(tc - 2),) + tuple(stage_war))
            evs.append(ev)
        d = tp.dma("gpsimd", lambda e, tc=tc, b=b: e.dma_start(
            out=yout[tc * 128:(tc + 1) * 128, :], in_=stage[b][:]), deps=tuple(evs))
        prev_dma[tc] = d
        st_ops.append(d)
    return st_ops



IN_SPECS = {
    "L0_attn_a_x": [T, C], "L0_attn_a_w_qkv": [C, 3 * C], "L0_attn_a_w_conv": [16, 1, 13, 1],
    "L0_attn_a_w_fc": [C, C], "L0_attn_a_b_fc": [C],
    "L0_attn_b_Wq": [C, C], "L0_attn_b_Wk": [C, C], "L0_attn_b_Wv": [C, C], "L0_attn_b_Wo": [C, C],
    "L0_ffn_v_first": [4, T, C], "L0_ffn_capital_shares": [3],
    "L0_ffn_ln1_w": [C], "L0_ffn_ln1_b": [C], "L0_ffn_ln2_w": [C], "L0_ffn_ln2_b": [C],
    "L0_ffn_Wr": [C, C], "L0_ffn_Wv": [C, C], "L0_ffn_Wo": [C, C], "L0_ffn_Ws": [C, C],
    "L0_ffn_K_rwkv": [2, C, HID], "L0_ffn_V_rwkv": [2, HID, C], "L0_ffn_conf_rwkv": [2, C],
    "L0_ffn_W1": [C, C], "L0_ffn_W2": [C, C], "L0_ffn_W3": [C, C],
    "L0_ffn_conf_trans": [C], "L0_ffn_w_diff": [C], "L0_ffn_W_aff": [C, 3],
    "L0_ffn_W_bridge": [C, C],
}


def build_program(parts=("attn_a", "ffn1", "attn_b", "ffn2")):
    nc = bass.Bass("TRN2", target_bir_lowering=False,
                   detect_race_conditions=RACE_DETECT)
    ins = {}
    for nm, shp in IN_SPECS.items():
        ins[nm] = nc.dram_tensor(nm, shp, F32, kind="ExternalInput").ap()
    yout = nc.dram_tensor("y", [T, C], F32, kind="ExternalOutput").ap()

    with contextlib.ExitStack() as es:
        k = K(nc, es)
        carrs = make_const_arrays()
        consts, cdeps = load_consts(k, carrs)
        bufs = alloc_bufs(k)
        scr = alloc_ln_scratch(k)
        sm = load_smalls(k, consts, cdeps, ins, bufs)
        globals()["LAST_BUFS"] = bufs
        globals()["LAST_SCR"] = scr
        X = bufs["X"]
        stage = [bufs["slab"].raw[0], bufs["slab2"].raw[0]]
        x_deps, lastev = emit_load_x(k, consts, cdeps, ins["L0_attn_a_x"], X, stage)
        bufs["slab"].prev_cast = [lastev[0]]
        bufs["slab2"].prev_cast = [lastev[1]]

        Wffn = {kk: ins["L0_ffn_" + kk] for kk in
                ("Wr", "Wv", "Wo", "Ws", "K_rwkv", "V_rwkv", "W1", "W2", "W3")}
        for part in parts:
            if part in ("ffn1", "ffn2"):
                x_deps = emit_ffn(k, consts, cdeps, scr, Wffn, sm, X, x_deps, bufs, tag=part)
            elif part == "attn_a":
                x_deps = emit_attn_a(k, consts, cdeps, scr, ins, sm, X, x_deps, bufs)
            elif part == "attn_b":
                x_deps = emit_attn_b(k, consts, cdeps, scr, ins, sm, X, x_deps, bufs)
        dump_outs = {}
        for dn in DUMPS:
            bl = bufs[dn] if dn in bufs else scr[dn]
            if not isinstance(bl, list):
                bl = [bl]
            dt_ = bl[0].ap().dtype
            dd = nc.dram_tensor("dbg_" + dn, [len(bl) * 128, bl[0].ap().shape[-1]], dt_,
                                kind="ExternalOutput").ap()
            for i3, b3 in enumerate(bl):
                k.tp.dma("gpsimd", lambda e, i3=i3, b3=b3, dd=dd: e.dma_start(
                    out=dd[i3 * 128:(i3 + 1) * 128 - (128 - b3.ap().shape[0])], in_=b3[:]),
                    deps=tuple(x_deps))
        st = emit_store_x(k, consts, cdeps, X, x_deps, yout, stage,
                          stage_war=tuple(p for p in bufs["slab"].prev_cast + bufs["slab2"].prev_cast
                                          if p is not None))
        k.tp.op("gpsimd", lambda e: None, deps=tuple(st))
        block = es.enter_context(nc.Block())
        k.tp.emit(nc, block, es)
    return nc


def kernel(**inputs):
    x_full = np.ascontiguousarray(inputs["L0_attn_a_x"], dtype=np.float32)
    nc = build_program()
    in_maps = []
    for b in range(N_CORES):
        m = {}
        for nm in IN_SPECS:
            if nm == "L0_attn_a_x":
                m[nm] = np.ascontiguousarray(x_full[b])
            else:
                m[nm] = np.ascontiguousarray(inputs[nm], dtype=np.float32)
        in_maps.append(m)
    res = run_bass_kernel_spmd(nc, in_maps, core_ids=list(range(N_CORES)))
    out = np.stack([res.results[b]["y"] for b in range(N_CORES)], axis=0)
    return out.astype(np.float32)


class PoolAlloc:

    def __init__(self, slots, init_war):
        self.slots = slots
        self.free = list(range(len(slots)))
        self.readers = {i: list(init_war) for i in range(len(slots))}

    def alloc(self, below=None):
        for j, i in enumerate(self.free):
            if below is None or i < below:
                return self.free.pop(j)
        raise RuntimeError("pool exhausted")

    def ap(self, i):
        return self.slots[i]

    def war(self, i):
        return tuple(self.readers[i])

    def wrote(self, i, opid):
        self.readers[i] = [opid]

    def read(self, i, opid):
        self.readers[i].append(opid)

    def release(self, i):
        self.free.append(i)


def emit_attn_a(k, consts, cdeps, scr, ins, sm, X, x_deps, bufs, tag="ata"):
    tp = k.tp
    nc = k.nc
    ARENA, ACT1, ACT2, GB, GS = bufs["ARENA"], bufs["ACT1"], bufs["ACT2"], bufs["GB"], bufs["GS"]
    SMW, SMWB, bbias, RL, SGT = bufs["SMW"], bufs["SMWB"], bufs["bbias"], bufs["RL"], bufs["SGT"]
    slab, slab2 = bufs["slab"], bufs["slab2"]
    wq = ins["L0_attn_a_w_qkv"]
    arena_war = list(bufs.get("arena_war", ()))
    a2war = list(bufs.get("act2_war", ()))
    gb_war = list(bufs.get("gb_war", ()))

    xa_ops = emit_ln(k, consts, cdeps, scr, X, x_deps, ACT1, tag=f"{tag}_ln",
                     out_war=bufs.get("act1_war", ()))
    if ATA_STOP == "ln":
        return list(x_deps) + list(xa_ops)

    slots = ([ARENA[6].ap()[:, i * 512:(i + 1) * 512] for i in range(4)]
             + [scr["ln_cast"][i][:] for i in range(2)]
             + [scr["ln_sq"][0][:]]
             + [GS[i].ap().bitcast(BF16)[:, j * 512:(j + 1) * 512] for i in range(2) for j in range(2)])
    PA = PoolAlloc(slots, arena_war + list(scr.get("ln_gate_row", ()))
                   + [d for d in (cdeps.get("_packT"), cdeps.get("_capb3"), cdeps.get("_lnp"))
                      if d is not None])
    iI = {}
    for nm in ("i7_bf", "i15_bf", "i13_bf"):
        i = PA.alloc()
        d = tp.dma("gpsimd", lambda e, nm=nm, i=i: e.dma_start(out=PA.ap(i), in_=k.cdram[nm].ap()),
                   deps=PA.war(i))
        PA.wrote(i, d)
        iI[nm] = i

    qlT = GB[0]
    klT = [RL[0], RL[1], SGT[0], SGT[1]]

    def klT_ap(hp):
        return klT[hp // 2][:, (hp % 2) * 256:(hp % 2) * 256 + 256]

    vpack = ARENA[7]

    def a_ap(h):
        return ACT2[h // 2].ap()[:, (h % 2) * 1024:(h % 2) * 1024 + 512]

    def aT_ap(h):
        return ACT2[h // 2].ap()[:, (h % 2) * 1024 + 512:(h % 2) * 1024 + 1024]

    cc_in = nc.dram_tensor(k.name("ccin"), [1, 1], F32, kind="Internal")
    cc_out = nc.dram_tensor(k.name("ccout"), [1, 1], F32, kind="Internal")
    ccsem = k.es.enter_context(nc.semaphore(k.name("ccsem")))

    qk_rd = {0: list(arena_war), 1: list(arena_war)}
    ql_rd = list(gb_war) + [cdeps.get("_packT"), cdeps.get("_capb3")]
    kl_rd = []
    a_rd = {h: list(a2war) for h in range(H)}
    gs1row_rd = []
    smw_rd = [cdeps.get("_lnp")]
    bias_rd = []
    vp_rd = list(arena_war)

    def qkv_project(hp, which, slot, scale, slabX, extra_rd):
        base = {"q": 0, "k": C, "v": 2 * C}[which] + hp * 128
        wbf, cst, bidx = slabX.load(lambda base=base: wq.rearrange(
            "(cc p) o -> p cc o", p=128)[:, :, base:base + 128])
        done = []
        first_rd = tuple(extra_rd)
        for nt in range(4):
            sl = slice(nt * 512, (nt + 1) * 512)
            bank = nt % 2
            calls = [(wbf[:, ccx * 128:(ccx + 1) * 128], ACT1[ccx][:, sl]) for ccx in range(NCC)]
            gmm = k.mm(bank, calls, deps=(cst,) + tuple(xa_ops))
            slabX.used(bidx, gmm)
            evo = k.rd(bank, "scalar", lambda e, slot=slot, sl=sl, bank=bank, scale=scale: e.activation(
                out=slot[:, sl], in_=k.pb[bank][:, 0:512], func=AF.Copy, scale=scale),
                deps=first_rd)
            first_rd = ()
            done.append(evo)
        return done

    def landmark_pool(src_slot, dst_ap, src_dep, war):
        acc = GS[0].ap()[:, 0:256]
        v8 = src_slot.ap().rearrange("p (m j) -> p j m", j=8)
        steps = [lambda e: e.tensor_add(acc, v8[:, 0, :], v8[:, 1, :])]
        for j in range(2, 8):
            steps.append(lambda e, j=j: e.tensor_add(acc, acc, v8[:, j, :]))
        steps.append(lambda e: e.tensor_scalar_mul(dst_ap, acc, 1.0 / 8.0))
        return k.vseq("vector", steps, deps=(src_dep,) + tuple(war))

    def norm_max(ap_, dep_, dst_col, width, slot):
        nonlocal smw_rd
        sq_out = PA.ap(slot)
        ops = []
        seq = None
        nblk = width // 512 if width >= 512 else 1
        for i3 in range(nblk):
            w3 = min(512, width)
            sq_o = tp.op("scalar", lambda e, ap_=ap_, i3=i3, w3=w3, sq_out=sq_out: e.activation(
                out=sq_out[:, 0:w3], in_=ap_[:, i3 * 512:i3 * 512 + w3], func=AF.Square),
                deps=(dep_, seq) + PA.war(slot))
            PA.wrote(slot, sq_o)
            gmm = k.mm(4, [(consts["headsel_bf"][:], sq_out[:, 0:w3])],
                       deps=(sq_o, cdeps["headsel_bf"]), out_ap=k.pb[4][0:2, 0:w3])
            PA.read(slot, gmm)
            dst = SMW[:, dst_col:dst_col + 1] if i3 == 0 else SMW[:, 30:31]
            red = k.rd(4, "vector", lambda e, dst=dst, w3=w3: e.reduce_max(
                dst[0:2, :], k.pb[4][0:2, 0:w3], AX.X), deps=tuple(smw_rd))
            smw_rd = []
            if i3 > 0:
                red = tp.op("vector", lambda e, dst_col=dst_col: e.tensor_max(
                    SMW[:, dst_col:dst_col + 1][0:2, :], SMW[:, dst_col:dst_col + 1][0:2, :],
                    SMW[:, 30:31][0:2, :]), force=(red,))
            seq = red
            ops.append(sq_o)
        smw_rd = [seq]
        return seq, ops

    def bias_broadcast(colA, colB, dst_col):
        nonlocal smw_rd, bias_rd
        cb = k.vseq("vector", [
            lambda e: e.tensor_add(SMW[:, 30:31][0:2, :], SMW[:, colA:colA + 1][0:2, :],
                                   SMW[:, colB:colB + 1][0:2, :]),
            lambda e: e.tensor_scalar_mul(SMW[:, 30:31][0:2, :], SMW[:, 30:31][0:2, :], -0.5),
            lambda e: e.tensor_copy(SMWB[0:2, 0:1], SMW[:, 30:31][0:2, :]),
        ], deps=tuple(smw_rd))[-1]
        smw_rd = [cb]
        outs = []
        for hh in range(2):
            gmm = k.mm(4, [(consts["rowsel_bf"][:, hh * 128:(hh + 1) * 128], SMWB[0:2, 0:1])],
                       deps=(cb, cdeps["rowsel_bf"]) + tuple(bias_rd), ncols=1)
            bias_rd = []
            outs.append(k.rd(4, "vector", lambda e, hh=hh, dst_col=dst_col: e.tensor_copy(
                bbias[:, dst_col + hh:dst_col + hh + 1], k.pb[4][:, 0:1])))
        return outs

    colmax_chain = None
    for hp in range(ATA_P1):
        qd = qkv_project(hp, "q", ARENA[0], 0.125, slab, qk_rd[0])
        kd = qkv_project(hp, "k", ARENA[1], 1.0, slab2, qk_rd[1])
        qlap = qlT.ap()[:, hp * 256:hp * 256 + 256]
        po1 = landmark_pool(ARENA[0], qlap, qd[-1], ql_rd)
        ql_rd = []
        po2 = landmark_pool(ARENA[1], klT_ap(hp), kd[-1], tuple(kl_rd) + (po1[-1],))
        kl_rd = []
        qk_rd[0] = [po1[-1]]
        qk_rd[1] = [po2[-1]]
        ql_done, kl_done = po1[-1], po2[-1]
        sqslot = PA.alloc(below=7)
        n1, sq1 = norm_max(qlap, ql_done, 26, 256, sqslot)
        n2, sq2 = norm_max(klT_ap(hp), kl_done, 27, 256, sqslot)
        PA.release(sqslot)
        b_ops = bias_broadcast(26, 27, 0)
        for hh in range(2):
            h = 2 * hp + hh
            r0 = 64 * hh
            g = None
            for mi in range(2):
                g = k.mm1(5, qlT.ap()[r0:r0 + 64, hp * 256 + mi * 128:hp * 256 + (mi + 1) * 128],
                          klT_ap(hp)[r0:r0 + 64, 0:256],
                          start=(mi == 0), stop=(mi == 1),
                          deps=(ql_done, kl_done),
                          out_ap=k.pb[5][:, mi * 256:(mi + 1) * 256])
            ql_rd.append(g)
            kl_rd.append(g)
            exps = []
            for mi in range(2):
                exps.append(k.rd(5, "scalar", lambda e, mi=mi, hh=hh, h=h: e.activation(
                    out=a_ap(h)[:, mi * 256:(mi + 1) * 256], in_=k.pb[5][:, mi * 256:(mi + 1) * 256],
                    func=AF.Exp, bias=bbias[:, hh:hh + 1],
                    accum_out=SMW[:, 28 + mi:29 + mi]),
                    deps=(b_ops[hh],) + tuple(a_rd[h]) + tuple(smw_rd)))
            a_rd[h] = []
            bias_rd.extend(exps)
            rs = k.vseq("vector", [
                lambda e: e.tensor_add(SMW[:, 28:29], SMW[:, 28:29], SMW[:, 29:30]),
                lambda e: e.reciprocal(SMW[:, 29:30], SMW[:, 28:29]),
            ], deps=tuple(exps))
            smw_rd = [rs[-1]]
            nrm = tp.op("scalar", lambda e, h=h: e.activation(
                out=a_ap(h), in_=a_ap(h), func=AF.Copy, scale=SMW[:, 29:30]),
                deps=(exps[-1], rs[-1]))
            smw_rd.append(nrm)
            gsum = k.mm(4, [(consts["ones_col_bf"][:], a_ap(h))], deps=(nrm, cdeps["ones_col_bf"]),
                        out_ap=k.pb[4][0:1, 0:512])
            a_rd[h].append(gsum)
            def _csr(e):
                e.tensor_copy(GS[0].ap()[0:1, 0:512], k.pb[4][0:1, 0:512])
                return e.tensor_add(GS[0].ap()[0:1, 0:256], GS[0].ap()[0:1, 0:256],
                                    GS[0].ap()[0:1, 256:512])
            csr = k.rd(4, "vector", _csr, deps=tuple(gs1row_rd))
            red = tp.op("vector", lambda e: e.reduce_max(
                SMW[:, 30:31][0:1, :], GS[0].ap()[0:1, 0:256], AX.X), force=(csr,))
            gs1row_rd = [red]
            if colmax_chain is None:
                colmax_chain = tp.op("vector", lambda e: e.tensor_copy(
                    SMW[:, 14:15][0:1, :], SMW[:, 30:31][0:1, :]), force=(red,))
            else:
                colmax_chain = tp.op("vector", lambda e: e.tensor_max(
                    SMW[:, 14:15][0:1, :], SMW[:, 14:15][0:1, :], SMW[:, 30:31][0:1, :]),
                    force=(red, colmax_chain))
            trs = []
            for tb, pairs in ((6, ((0, 0), (0, 1))), (7, ((1, 0), (1, 1)))):
                for pi, (bi, bj) in enumerate(pairs):
                    t = k.transpose_into(tb, pi * 128,
                                         a_ap(h)[:, bj * 256 + bi * 128:bj * 256 + (bi + 1) * 128],
                                         consts["ident_bf"][:],
                                         deps=(nrm, cdeps["ident_bf"]), first=(pi == 0))
                    a_rd[h].append(t)
                ev = k.rd(tb, "scalar", lambda e, tb=tb, h=h: e.copy(
                    out=aT_ap(h)[:, (tb - 6) * 256:(tb - 6) * 256 + 256],
                    in_=k.pb[tb].ap().bitcast(BF16)[:, 0:256]),
                    deps=tuple(a_rd[h][-1:]))
                trs.append(ev)
            a_rd[h].extend(trs)

    if ATA_STOP == "pass1":
        return list(x_deps) + [colmax_chain]
    dcc1 = tp.dma("gpsimd", lambda e: e.dma_start(out=cc_in.ap(), in_=SMW[0:1, 14:15]),
                  deps=(colmax_chain,))

    def _cc(e):
        if ATA_LOCAL:
            e.dma_start(out=cc_out.ap(), in_=cc_in.ap()).then_inc(ccsem, 16)
            return None
        e.collective_compute("AllReduce", ALU.max, replica_groups=[list(range(N_CORES))],
                             ins=[cc_in.ap()], outs=[cc_out.ap()]).then_inc(ccsem, 1)
        return None
    ccop = tp.op("gpsimd", _cc, deps=(dcc1,))

    def _ccback(e):
        e.wait_ge(ccsem, 16 if ATA_LOCAL else 1)
        return e.dma_start(out=SMW[0:1, 30:31], in_=cc_out.ap())
    ccb = tp.dma("gpsimd", _ccback, deps=())
    sv = k.vseq("vector", [
        lambda e: e.reciprocal(SMW[:, 31:32][0:1, :], SMW[:, 30:31][0:1, :]),
        lambda e: e.tensor_copy(SMWB[0:1, 1:2], SMW[:, 31:32][0:1, :]),
    ], deps=(ccb,))[-1]
    gmm = k.mm(4, [(consts["ones_row_bf"][0:1, 0:128], SMWB[0:1, 1:2])],
               deps=(sv,) + tuple(bias_rd), ncols=1)
    bias_rd = []
    sinv_op = k.rd(4, "vector", lambda e: e.tensor_copy(bbias[:, 6:7], k.pb[4][:, 0:1]))
    if ATA_STOP == "cc":
        return list(x_deps) + [sinv_op]

    fc_rd = list(a2war)
    a23_rd = []
    a45_rd = []
    for hp in range(NCC):
        qd = qkv_project(hp, "q", ARENA[0], 0.125, slab, qk_rd[0])
        kd = qkv_project(hp, "k", ARENA[1], 1.0, slab2, qk_rd[1])
        qk_rd[0] = []
        qk_rd[1] = []
        base = 2 * C + hp * 128
        vbf, vcst, vbidx = slab.load(lambda base=base: wq.rearrange(
            "(cc p) o -> p cc o", p=128)[:, :, base:base + 128])
        vdone = []
        for tc in range(16):
            calls = [(ACT1[ccx][:, tc * 128:(tc + 1) * 128], vbf[:, ccx * 128:(ccx + 1) * 128])
                     for ccx in range(NCC)]
            gmmv = k.mm(2, calls, deps=(vcst,) + tuple(xa_ops), out_ap=k.pb[2][:, 0:128])
            slab.used(vbidx, gmmv)
            ev = k.rd(2, "scalar", lambda e, tc=tc: e.copy(
                out=vpack.ap()[:, tc * 128:(tc + 1) * 128], in_=k.pb[2][:, 0:128]),
                deps=tuple(vp_rd))
            vp_rd = []
            vdone.append(ev)
        sqslot = PA.alloc()
        nq, sqo1 = norm_max(ARENA[0].ap(), qd[-1], 24, 2048, sqslot)
        nk, sqo2 = norm_max(ARENA[1].ap(), kd[-1], 25, 2048, sqslot)
        nql, sqo3 = norm_max(qlT.ap()[:, hp * 256:hp * 256 + 256], sinv_op, 26, 256, sqslot)
        nkl, sqo4 = norm_max(klT_ap(hp), sinv_op, 27, 256, sqslot)
        PA.release(sqslot)
        bM1 = bias_broadcast(24, 27, 0)
        bM3 = bias_broadcast(26, 25, 2)
        if ATA_P2 == "qkv":
            qk_rd[0] = qd[-1:]
            qk_rd[1] = kd[-1:]
            vp_rd = vdone[-1:]
            continue
        zslots = {}
        for hh in range(2):
            h = 2 * hp + hh
            pz = PA.alloc()
            pzT = PA.alloc()
            z0 = tp.op("scalar", lambda e, h=h, pz=pz: e.activation(
                out=PA.ap(pz), in_=aT_ap(h), func=AF.Copy, scale=bbias[:, 6:7]),
                deps=(sinv_op,) + PA.war(pz) + tuple(a_rd[h]))
            PA.wrote(pz, z0)
            z0T = tp.op("scalar", lambda e, h=h, pzT=pzT: e.activation(
                out=PA.ap(pzT), in_=a_ap(h), func=AF.Copy, scale=bbias[:, 6:7]),
                deps=(sinv_op,) + PA.war(pzT))
            PA.wrote(pzT, z0T)
            pazT = PA.alloc()
            pw1 = PA.alloc()
            pw1T = PA.alloc()
            pw2 = PA.alloc()
            pw2T = PA.alloc()

            def mm4(bank, lhs_store, rhs_store, deps):
                gg = None
                for mi in range(2):
                    for kc in range(2):
                        gg = k.mm1(bank, lhs_store[:, kc * 256 + mi * 128:kc * 256 + (mi + 1) * 128],
                                   rhs_store[:, kc * 256:(kc + 1) * 256],
                                   start=(mi == 0 and kc == 0), stop=(mi == 1 and kc == 1),
                                   deps=deps,
                                   out_ap=k.pb[bank][:, mi * 256:(mi + 1) * 256])
                return gg

            zdep = (z0, z0T)
            for it in range(ATA_NS):
                g_az = mm4(5, aT_ap(h), PA.ap(pz), zdep + tuple(a_rd[h]))
                a_rd[h].append(g_az)
                if ATA_NSOP <= 1:
                    zdep = (g_az, z0T)
                    break
                g_azT = mm4(6, PA.ap(pz), aT_ap(h), zdep)
                PA.read(pz, g_azT)
                a_rd[h].append(g_azT)
                w1 = k.rd(5, "vector", lambda e, pw1=pw1: e.tensor_sub(
                    PA.ap(pw1), PA.ap(iI["i7_bf"]), k.pb[5][:, 0:512]), deps=PA.war(pw1))
                PA.wrote(pw1, w1)
                if ATA_NSOP <= 2:
                    zdep = (w1, g_azT)
                    break
                if ATA_NSOP == 21:
                    azT = k.rd(6, "scalar", lambda e, pazT=pazT: e.copy(
                        out=PA.ap(pazT), in_=k.pb[6][:, 0:512]), deps=PA.war(pazT))
                    PA.wrote(pazT, azT)
                    zdep = (w1, azT)
                    break
                if ATA_NSOP == 22:
                    w1T = k.rd(6, "vector", lambda e, pw1T=pw1T: e.tensor_sub(
                        PA.ap(pw1T), PA.ap(iI["i7_bf"]), k.pb[6][:, 0:512]), deps=PA.war(pw1T))
                    PA.wrote(pw1T, w1T)
                    zdep = (w1, w1T)
                    break
                azT = k.rd(6, "scalar", lambda e, pazT=pazT: e.copy(
                    out=PA.ap(pazT), in_=k.pb[6][:, 0:512]), deps=PA.war(pazT))
                PA.wrote(pazT, azT)
                w1T = k.rd(6, "vector", lambda e, pw1T=pw1T: e.tensor_sub(
                    PA.ap(pw1T), PA.ap(iI["i7_bf"]), k.pb[6][:, 0:512]),
                    deps=PA.war(pw1T) + (azT,))
                PA.wrote(pw1T, w1T)
                PA.read(iI["i7_bf"], w1)
                PA.read(iI["i7_bf"], w1T)
                if ATA_NSOP <= 3:
                    zdep = (w1, w1T)
                    break
                g_m2 = mm4(5, PA.ap(pazT), PA.ap(pw1), (azT, w1))
                PA.read(pazT, g_m2)
                PA.read(pw1, g_m2)
                if ATA_NSOP <= 4:
                    zdep = (g_m2, w1T)
                    break
                g_m2T = mm4(6, PA.ap(pw1), PA.ap(pazT), (azT, w1T, w1))
                PA.read(pazT, g_m2T)
                PA.read(pw1, g_m2T)
                w2 = k.rd(5, "vector", lambda e, pw2=pw2: e.tensor_sub(
                    PA.ap(pw2), PA.ap(iI["i15_bf"]), k.pb[5][:, 0:512]), deps=PA.war(pw2))
                PA.wrote(pw2, w2)
                if ATA_NSOP <= 5:
                    zdep = (w2, g_m2T)
                    break
                w2T = k.rd(6, "vector", lambda e, pw2T=pw2T: e.tensor_sub(
                    PA.ap(pw2T), PA.ap(iI["i15_bf"]), k.pb[6][:, 0:512]), deps=PA.war(pw2T))
                PA.wrote(pw2T, w2T)
                PA.read(iI["i15_bf"], w2)
                PA.read(iI["i15_bf"], w2T)
                g_m3 = mm4(5, PA.ap(pazT), PA.ap(pw2), (w2,))
                PA.read(pazT, g_m3)
                PA.read(pw2, g_m3)
                if ATA_NSOP <= 6:
                    zdep = (g_m3, w2T)
                    break
                g_m3T = mm4(6, PA.ap(pw2), PA.ap(pazT), (w2, w2T))
                PA.read(pazT, g_m3T)
                PA.read(pw2, g_m3T)
                if ATA_NSOP <= 7:
                    zdep = (g_m3, g_m3T)
                    break
                w3 = k.rd(5, "vector", lambda e, pw1=pw1: e.tensor_sub(
                    PA.ap(pw1), PA.ap(iI["i13_bf"]), k.pb[5][:, 0:512]), deps=PA.war(pw1))
                PA.wrote(pw1, w3)
                w3T = k.rd(6, "vector", lambda e, pw1T=pw1T: e.tensor_sub(
                    PA.ap(pw1T), PA.ap(iI["i13_bf"]), k.pb[6][:, 0:512]), deps=PA.war(pw1T))
                PA.wrote(pw1T, w3T)
                PA.read(iI["i13_bf"], w3)
                PA.read(iI["i13_bf"], w3T)
                if ATA_NSOP <= 8:
                    zdep = (w3, w3T)
                    break
                g_z = mm4(5, PA.ap(pzT), PA.ap(pw1), (w3,))
                PA.read(pzT, g_z)
                PA.read(pw1, g_z)
                if ATA_NSOP <= 9:
                    zdep = (g_z, w3T)
                    break
                g_zT = mm4(6, PA.ap(pw1), PA.ap(pzT), (w3, w3T))
                PA.read(pzT, g_zT)
                PA.read(pw1T, g_zT)
                zn = k.rd(5, "scalar", lambda e, pz=pz: e.activation(
                    out=PA.ap(pz), in_=k.pb[5][:, 0:512], func=AF.Copy, scale=0.25),
                    deps=PA.war(pz))
                PA.wrote(pz, zn)
                znT = k.rd(6, "scalar", lambda e, pzT=pzT: e.activation(
                    out=PA.ap(pzT), in_=k.pb[6][:, 0:512], func=AF.Copy, scale=0.25),
                    deps=PA.war(pzT))
                PA.wrote(pzT, znT)
                zdep = (zn, znT)
            zslots[hh] = (pzT, zdep[1])
            PA.release(pz)
            for p_ in (pazT, pw1, pw1T, pw2, pw2T):
                PA.release(p_)
        if ATA_P2 == "ns":
            for hh2 in range(2):
                PA.release(zslots[hh2][0])
            qk_rd[0] = qd[-1:]
            qk_rd[1] = kd[-1:]
            vp_rd = vdone[-1:]
            continue
        for hh in range(2):
            h = 2 * hp + hh
            r0 = 64 * hh
            pzT, zT_done = zslots[hh]
            at1_rd = []
            for mi in range(2):
                for nt in range(4):
                    sl = slice(nt * 512, (nt + 1) * 512)
                    bank = nt % 2
                    g1 = k.mm1(bank, klT_ap(hp)[r0:r0 + 64, mi * 128:(mi + 1) * 128],
                               ARENA[0].ap()[r0:r0 + 64, sl], start=True, stop=True,
                               deps=(qd[-1], sinv_op))
                    qk_rd[0].append(g1)
                    e1 = k.rd(bank, "scalar", lambda e, mi=mi, sl=sl, bank=bank, hh=hh: e.activation(
                        out=ARENA[2 + mi].ap()[:, sl], in_=k.pb[bank][:, 0:512],
                        func=AF.Exp, bias=bbias[:, 0 + hh:1 + hh]),
                        deps=(bM1[hh],) + tuple(a23_rd))
                    bias_rd.append(e1)
                    a23_rd = []
                    at1_rd.append(e1)
            if ATA_P2 == "sims1":
                PA.release(pzT)
                continue
            at3_rd = []
            for kb in range(16):
                bank = 2 + kb % 2
                g3 = k.mm1(bank, ARENA[1].ap()[r0:r0 + 64, kb * 128:(kb + 1) * 128],
                           qlT.ap()[r0:r0 + 64, hp * 256:hp * 256 + 256], start=True, stop=True,
                           deps=(kd[-1], sinv_op))
                qk_rd[1].append(g3)
                ql_rd.append(g3)
                e3 = k.rd(bank, "scalar", lambda e, kb=kb, bank=bank, hh=hh: e.activation(
                    out=ARENA[4 + kb // 8].ap()[:, (kb % 8) * 256:(kb % 8) * 256 + 256],
                    in_=k.pb[bank][:, 0:256], func=AF.Exp, bias=bbias[:, 2 + hh:3 + hh]),
                    deps=(bM3[hh],) + tuple(a45_rd))
                bias_rd.append(e3)
                a45_rd = []
                at3_rd.append(e3)
            if ATA_P2 == "sims3":
                PA.release(pzT)
                continue
            gr = None
            first = True
            for mi in range(2):
                for kb in range(16):
                    lhs = ARENA[4 + kb // 8].ap()[:, (kb % 8) * 256 + mi * 128:(kb % 8) * 256 + (mi + 1) * 128]
                    gr = k.mm1(7, lhs, vpack.ap()[:, kb * 128 + r0:kb * 128 + r0 + 64],
                               start=first, stop=False, deps=(at3_rd[kb], vdone[kb]),
                               out_ap=k.pb[7][:, mi * 65:mi * 65 + 64])
                    first = False
                    gr = k.mm1(7, lhs, consts["ones_col_bf"][:],
                               start=False, stop=(mi == 1 and kb == 15),
                               deps=(cdeps["ones_col_bf"],),
                               out_ap=k.pb[7][:, mi * 65 + 64:mi * 65 + 65])
                    a45_rd.append(gr)
            pr3 = PA.alloc()
            r3ev = []
            for mi in range(2):
                rc = k.rd(7, "vector", lambda e, mi=mi: e.reciprocal(
                    SMW[:, 28:29], k.pb[7][:, mi * 65 + 64:mi * 65 + 65]),
                    deps=tuple(smw_rd))
                smw_rd = []
                ev = k.rd(7, "scalar", lambda e, mi=mi, pr3=pr3: e.activation(
                    out=PA.ap(pr3)[:, mi * 64:(mi + 1) * 64], in_=k.pb[7][:, mi * 65:mi * 65 + 64],
                    func=AF.Copy, scale=SMW[:, 28:29]),
                    deps=(rc,) + (PA.war(pr3) if mi == 0 else ()))
                smw_rd = [ev]
                r3ev.append(ev)
            PA.wrote(pr3, r3ev[-1])
            if ATA_P2 == "r3":
                PA.release(pr3)
                PA.release(pzT)
                continue
            pza = PA.alloc()
            zev = []
            first = True
            for mi in range(2):
                gz = None
                for kc in range(2):
                    gz = k.mm1(7, PA.ap(pzT)[:, kc * 256 + mi * 128:kc * 256 + (mi + 1) * 128],
                               PA.ap(pr3)[:, kc * 64:(kc + 1) * 64],
                               start=first, stop=(mi == 1 and kc == 1),
                               deps=(zT_done, r3ev[-1]),
                               out_ap=k.pb[7][:, mi * 65:mi * 65 + 64])
                    first = False
                    PA.read(pzT, gz)
                    PA.read(pr3, gz)
                ze = k.rd(7, "scalar", lambda e, mi=mi, pza=pza: e.copy(
                    out=PA.ap(pza)[:, mi * 65:mi * 65 + 64], in_=k.pb[7][:, mi * 65:mi * 65 + 64]),
                    deps=(PA.war(pza) if mi == 0 else ()))
                zev.append(ze)
            def _mz(e, pza=pza):
                e.memset(PA.ap(pza)[:, 64:65], 1.0)
                return e.memset(PA.ap(pza)[:, 129:130], 1.0)
            mzo = tp.op("vector", _mz, deps=PA.war(pza) + (zev[0],))
            PA.wrote(pza, mzo)
            for ze in zev:
                PA.read(pza, ze)
            if ATA_P2 == "z":
                PA.release(pr3)
                PA.release(pza)
                PA.release(pzT)
                continue
            prb = PA.alloc()
            for nt in range(4):
                sl = slice(nt * 512, (nt + 1) * 512)
                bank = nt % 2
                go = None
                for kc in range(2):
                    go = k.mm1(bank, PA.ap(pza)[:, kc * 65:(kc + 1) * 65],
                               ARENA[2 + kc].ap()[:, sl],
                               start=(kc == 0), stop=(kc == 1),
                               deps=(mzo, zev[-1], at1_rd[7]),
                               out_ap=k.pb[bank][0:65, 0:512])
                    PA.read(pza, go)
                    a23_rd.append(go)
                def _rc2(e, bank=bank):
                    with nc.allow_low_precision(reason="softmax 1/sum bf16"):
                        return e.reciprocal(bufs["rrow"][0:1, 0:512], k.pb[bank][64:65, 0:512])
                rc2 = k.rd(bank, "vector", _rc2, deps=tuple(gs1row_rd))
                gs1row_rd = []
                bmm = k.mm(4, [(consts["ones_row_bf"][0:1, 0:64], bufs["rrow"][0:1, 0:512])],
                           deps=(rc2,), out_ap=k.pb[4][0:64, 0:512])
                rbev = k.rd(4, "scalar", lambda e, prb=prb: e.copy(
                    out=PA.ap(prb)[0:64, :], in_=k.pb[4][0:64, 0:512]), deps=PA.war(prb))
                PA.wrote(prb, rbev)
                gs1row_rd = [bmm]
                evf = k.rd(bank, "vector", lambda e, bank=bank, hp=hp, r0=r0, sl=sl, prb=prb: e.tensor_mul(
                    ACT2[hp][r0:r0 + 64, sl], k.pb[bank][0:64, 0:512], PA.ap(prb)[0:64, 0:512]),
                    deps=(rbev,) + tuple(a_rd[2 * hp]) + tuple(a_rd[2 * hp + 1]))
                a_rd[2 * hp] = []
                a_rd[2 * hp + 1] = []
                PA.read(prb, evf)
                fc_rd.append(evf)
            PA.release(pr3)
            PA.release(pza)
            PA.release(prb)
            PA.release(pzT)
        kl_rd = []
        ql_rd = []
        vp_rd = []

    if ATA_STOP == "pass2":
        return list(x_deps) + list(fc_rd) + list(qk_rd[0]) + list(qk_rd[1])
    taps = SMW
    wc2 = GS[1]
    conv_done = []
    x_new = list(x_deps)
    combined_ops = {}
    for ccx in range(NCC):
        dwc = tp.dma("gpsimd", lambda e, ccx=ccx: e.dma_start(
            out=wc2[0:2, 0:13],
            in_=ins["L0_attn_a_w_conv"].rearrange("h a kk b -> (a b) h kk")[0, 2 * ccx:2 * ccx + 2, :]),
            deps=tuple(conv_done[-1:]) + tuple(fc_rd[-4:]))
        gt = k.mm(4, [(consts["headselT_f32"][:], wc2[0:2, 0:13])],
                  deps=(dwc, cdeps["headselT_f32"]), ncols=13)
        tev = k.rd(4, "vector", lambda e: e.tensor_copy(taps[:, 0:13], k.pb[4][:, 0:13]),
                   deps=(cdeps.get("_lnp"),))
        tadd = tp.op("vector", lambda e: e.tensor_scalar_add(taps[:, 6:7], taps[:, 6:7], 1.0),
                     force=(tev,))
        c0 = tp.op("vector", lambda e, ccx=ccx: e.tensor_single_scalar(
            ARENA[ccx][:], ACT2[ccx][:], taps[:, 6:7], ALU.mult),
            deps=(tadd,) + tuple(fc_rd) + tuple(qk_rd[0]) + tuple(qk_rd[1]) + tuple(vp_rd))
        prev = c0
        scr_rd = {0: tuple(conv_done[-1:]), 1: ()}
        js = [j for j in range(13) if j != 6]
        for ji, j in enumerate(js):
            dshift = j - 6
            slo = max(0, dshift)
            shi = 2048 + min(0, dshift)
            dlo = max(0, -dshift)
            sb_ = ji % 2
            scratch = GB[0] if sb_ == 0 else ACT1[ccx]
            if sb_ == 0:
                ts_o = tp.op("gpsimd", lambda e, ccx=ccx, j=j, slo=slo, shi=shi, scratch=scratch: e.tensor_single_scalar(
                    scratch[:, 0:shi - slo], ACT2[ccx][:, slo:shi], taps[:, j:j + 1], ALU.mult),
                    deps=(tadd,) + scr_rd[sb_])
            else:
                ts_o = tp.op("scalar", lambda e, ccx=ccx, j=j, slo=slo, shi=shi, scratch=scratch: e.activation(
                    out=scratch[:, 0:shi - slo], in_=ACT2[ccx][:, slo:shi],
                    func=AF.Copy, scale=taps[:, j:j + 1]),
                    deps=(tadd,) + scr_rd[sb_])
            ad_o = tp.op("vector", lambda e, ccx=ccx, dlo=dlo, slo=slo, shi=shi, scratch=scratch: e.tensor_add(
                ARENA[ccx][:, dlo:dlo + shi - slo], ARENA[ccx][:, dlo:dlo + shi - slo],
                scratch[:, 0:shi - slo]), deps=(ts_o, prev), force=(c0,) if ji == 0 else ())
            scr_rd[sb_] = (ad_o,)
            prev = ad_o
        conv_done.append(prev)
        combined_ops[ccx] = prev

    if ATA_STOP == "conv":
        return list(x_deps) + list(conv_done)
    wfc_done = []
    def mk_fc(mi, nt, bank):
        sl = slice(nt * 512, (nt + 1) * 512)
        o = k.rd(bank, "vector", lambda e, mi=mi, sl=sl, bank=bank: e.scalar_tensor_tensor(
            X[mi][:, sl], X[mi][:, sl], sm["bfc"][:, mi:mi + 1], k.pb[bank][:, 0:512],
            ALU.add, ALU.add), deps=(x_new[mi], cdeps.get("_lnp")))
        x_new[mi] = o
        wfc_done.append(o)
    groups = emit_ccpass(k, slab, ins["L0_attn_a_w_fc"], [ARENA[i] for i in range(NCC)],
                         list(combined_ops.values()), mk_fc)
    bufs["arena_war"] = tuple(groups[-8:])
    bufs["act2_war"] = tuple(combined_ops.values())
    bufs["gb_war"] = tuple(conv_done)
    bufs["act1_war"] = tuple(qk_rd[0]) + tuple(qk_rd[1]) + tuple(conv_done)
    scr["ln_prev_users"] = tuple(scr.get("ln_prev_users", ())) + tuple(
        v for rd in PA.readers.values() for v in rd[-1:])
    return x_new


def emit_attn_b(k, consts, cdeps, scr, ins, sm, X, x_deps, bufs, tag="atb"):
    tp = k.tp
    ARENA, ACT1, ACT2, GB = bufs["ARENA"], bufs["ACT1"], bufs["ACT2"], bufs["GB"]
    SMW, SMWB, bbias = bufs["SMW"], bufs["SMWB"], bufs["bbias"]
    slab = bufs["slab"]
    arena_war = list(bufs.get("arena_war", ()))

    xb_ops = emit_ln(k, consts, cdeps, scr, X, x_deps, ACT1, tag=f"{tag}_ln",
                     out_war=bufs.get("act1_war", ()))

    ropeC = [ACT2[4 + nt].ap()[:, 0:512] for nt in range(4)]
    ropeS = [ACT2[4 + nt].ap()[:, 512:1024] for nt in range(4)]
    rope_deps = []
    a2war = tuple(bufs.get("act2_war", ()))
    for nt in range(4):
        rope_deps.append(tp.dma("gpsimd", lambda e, nt=nt: e.dma_start(
            out=ropeC[nt], in_=k.cdram["ropeC_bf"].ap()[:, nt * 512:(nt + 1) * 512]), deps=a2war))
        rope_deps.append(tp.dma("gpsimd", lambda e, nt=nt: e.dma_start(
            out=ropeS[nt], in_=k.cdram["ropeS_bf"].ap()[:, nt * 512:(nt + 1) * 512]), deps=a2war))

    wt = ACT2[2]
    scr3 = ACT2[3]
    rrow = bufs["rrow"]
    vpack = [ARENA[4 + i] for i in range(4)]
    PT = GB[0]
    slot_rd = {i: list(arena_war) for i in range(4)}
    vpack_rd = list(arena_war)
    wt_rd = list(a2war)
    scr3_rd = list(a2war)
    pt_rd = list(bufs.get("gb_war", ()))
    bias_rd = []
    x_new = list(x_deps)

    def wt_block(m, ccx):
        off = m * 1024 + (ccx // 4) * 512 + (ccx % 4) * 128
        return wt[:, off:off + 128]

    for Q in range(ATB_QUARTERS):
        if Q > 0 and ATB_BARRIER:
            tp.ambient = ()
            tp.ambient = tuple(tp.last_per_engine())
        def wtrans_qkv(wdram):
            ops = []
            for m in range(2):
                wbf, cst, bidx = slab.load(
                    lambda m=m, wdram=wdram, Q=Q: wdram[256 * Q + m * 128:256 * Q + (m + 1) * 128, :])
                for g4 in range(2):
                    tb = 6 + g4
                    for j in range(4):
                        ccx = g4 * 4 + j
                        trj = k.transpose_into(tb, j * 128, wbf[:, ccx * 128:(ccx + 1) * 128],
                                               consts["ident_bf"][:],
                                               deps=(cst, cdeps["ident_bf"]), first=(j == 0))
                        slab.used(bidx, trj)
                    ev = k.rd(tb, "scalar", lambda e, m=m, g4=g4, tb=tb: e.copy(
                        out=wt[:, m * 1024 + g4 * 512:m * 1024 + (g4 + 1) * 512],
                        in_=k.pb[tb].ap().bitcast(BF16)[:, 0:512]), deps=tuple(wt_rd))
                    ops.append(ev)
            return ops

        qk_ready = {}
        for wi, wname in enumerate(("L0_attn_b_Wq", "L0_attn_b_Wk")):
            wops = wtrans_qkv(ins[wname])
            wt_rd = []
            for m in range(2):
                slot = ARENA[wi * 2 + m]
                for nt in range(4):
                    sl = slice(nt * 512, (nt + 1) * 512)
                    bank = nt % 2
                    calls = [(wt_block(m, ccx), ACT1[ccx][:, sl]) for ccx in range(NCC)]
                    gmm = k.mm(bank, calls, deps=tuple(wops) + tuple(xb_ops))
                    wt_rd.append(gmm)
                    evo = k.rd(bank, "scalar", lambda e, slot=slot, sl=sl, bank=bank, wi=wi: e.activation(
                        out=slot[:, sl], in_=k.pb[bank][:, 0:512], func=AF.Copy,
                        scale=(0.125 if wi == 0 else 1.0)),
                        deps=tuple(slot_rd[wi * 2 + m]))
                    slot_rd[wi * 2 + m] = []
                    pb2 = 2 + nt % 2
                    pmm = k.mm(pb2, [(consts["perm_bf"][:], slot[:, sl])],
                               deps=(evo, cdeps["perm_bf"]))
                    t1o = tp.op("vector", lambda e, slot=slot, sl=sl, nt=nt: e.tensor_mul(
                        scr3[:, 1536:2048], slot[:, sl], ropeC[nt]),
                        deps=(evo, rope_deps[nt * 2]) + tuple(scr3_rd))
                    scr3_rd = []
                    t2o = k.rd(pb2, "vector", lambda e, sl=sl, nt=nt, pb2=pb2: e.tensor_mul(
                        PT[:, 0:512], k.pb[pb2][:, 0:512], ropeS[nt]),
                        deps=(rope_deps[nt * 2 + 1],) + tuple(pt_rd))
                    pt_rd = []
                    t3o = tp.op("vector", lambda e, slot=slot, sl=sl: e.tensor_add(
                        slot[:, sl], scr3[:, 1536:2048], PT[:, 0:512]), force=(t1o, t2o))
                    scr3_rd.append(t3o)
                    pt_rd.append(t3o)
                    slot_rd[wi * 2 + m] = [t3o]
            qk_ready[wi * 2] = slot_rd[wi * 2][-1]
            qk_ready[wi * 2 + 1] = slot_rd[wi * 2 + 1][-1]

        vops = wtrans_qkv(ins["L0_attn_b_Wv"])
        wt_rd = []
        vdone = []
        for tc in range(16):
            vp = vpack[tc // 4]
            voff = (tc % 4) * 260
            bank = 4
            for m in range(2):
                cl = [(ACT1[ccx][:, tc * 128:(tc + 1) * 128], wt_block(m, ccx)) for ccx in range(NCC)]
                gmm = k.mm(bank, cl,
                           deps=tuple(vops) + tuple(xb_ops),
                           out_ap=k.pb[bank][:, m * 128:(m + 1) * 128])
                wt_rd.append(gmm)
            ev = k.rd(bank, "scalar", lambda e, vp=vp, voff=voff, bank=bank: e.copy(
                out=vp.ap()[:, voff:voff + 260].rearrange("p (h d) -> p h d", h=4)[:, :, 0:64],
                in_=k.pb[bank][:, 0:256].rearrange("p (h d) -> p h d", h=4)),
                deps=tuple(vpack_rd))
            vdone.append(ev)
        vpack_rd = []
        for i in range(4):
            vdone.append(tp.op("vector", lambda e, i=i: e.memset(
                vpack[i].ap()[:, 0:1040].rearrange("p (c h d) -> p c h d", c=4, h=4)[:, :, :, 64:65], 1.0),
                deps=tuple(arena_war)))
        arena_war = []

        woops = []
        for ccx in range(NCC):
            wbf, cst, bidx = slab.load(lambda ccx=ccx: ins["L0_attn_b_Wo"][ccx * 128:(ccx + 1) * 128, :])
            tb = 6 + ccx % 2
            for m in range(2):
                trj = k.transpose_into(tb, m * 128, wbf[:, 256 * Q + m * 128:256 * Q + (m + 1) * 128],
                                       consts["ident_bf"][:], deps=(cst, cdeps["ident_bf"]),
                                       first=(m == 0))
                slab.used(bidx, trj)
            ev = k.rd(tb, "scalar", lambda e, ccx=ccx, tb=tb: e.copy(
                out=wt[:, ccx * 256:(ccx + 1) * 256],
                in_=k.pb[tb].ap().bitcast(BF16)[:, 0:256]), deps=tuple(wt_rd))
            woops.append(ev)
        wt_rd = []

        def woT_block(m2, mi):
            return wt[:, mi * 256 + m2 * 128:mi * 256 + (m2 + 1) * 128]

        bias_ops = [None] * 4
        for m in range(2):
            accq, acck, tmpc = SMW[:, 24:25], SMW[:, 25:26], SMW[:, 26:27]
            accs = []
            for wi in range(2):
                acc = accq if wi == 0 else acck
                slot = ARENA[wi * 2 + m]
                seq = None
                for nt in range(4):
                    sq_o = tp.op("scalar", lambda e, slot=slot, nt=nt: e.activation(
                        out=scr3[:, 0:512], in_=slot[:, nt * 512:(nt + 1) * 512], func=AF.Square),
                        deps=(qk_ready[wi * 2 + m],) + ((seq,) if seq else ()) + tuple(scr3_rd))
                    scr3_rd = []
                    gmm = k.mm(5, [(consts["headsel_bf"][:], scr3.ap()[:, 0:512])],
                               deps=(sq_o, cdeps["headsel_bf"]), out_ap=k.pb[5][0:2, 0:512])
                    slot_rd[wi * 2 + m].append(sq_o)
                    dst = acc if nt == 0 else tmpc
                    red = k.rd(5, "vector", lambda e, dst=dst: e.reduce_max(
                        dst[0:2, :], k.pb[5][0:2, 0:512], AX.X))
                    if nt > 0:
                        red = tp.op("vector", lambda e, acc=acc: e.tensor_max(
                            acc[0:2, :], acc[0:2, :], tmpc[0:2, :]), force=(red,))
                    seq = red
                    scr3_rd.append(gmm)
                accs.append(seq)
            cb = k.vseq("vector", [
                lambda e: e.tensor_add(SMW[:, 26:27][0:2, :], SMW[:, 24:25][0:2, :], SMW[:, 25:26][0:2, :]),
                lambda e: e.tensor_scalar_mul(SMW[:, 26:27][0:2, :], SMW[:, 26:27][0:2, :], -0.5),
                lambda e: e.tensor_copy(SMWB[0:2, 0:1], SMW[:, 26:27][0:2, :]),
            ], deps=tuple(accs))[-1]
            for hh in range(2):
                gmm = k.mm(5, [(consts["rowsel_bf"][:, hh * 128:(hh + 1) * 128], SMWB[0:2, 0:1])],
                           deps=(cb, cdeps["rowsel_bf"]) + tuple(bias_rd), ncols=1)
                bias_rd = []
                bias_ops[m * 2 + hh] = k.rd(5, "vector", lambda e, m=m, hh=hh: e.tensor_copy(
                    bbias[:, m * 2 + hh:m * 2 + hh + 1], k.pb[5][:, 0:1]))

        ot_evacs = []
        prev_norm = {}
        pt_rd_tile = {nt: list(pt_rd) for nt in range(4)}
        pt_rd = []
        for hi in range(4):
            m = hi // 2
            r0 = 64 * (hi % 2)
            qs = ARENA[0 + m]
            ks = ARENA[2 + m]
            pv_group = [None] * 4
            for kb in range(16):
                nt0 = kb // 4
                for nt in range(nt0, 4):
                    sb = 4 + nt % 2
                    colskip = kb * 128 - 512 * nt if nt == nt0 else 0
                    smm = k.mm1(sb, ks[r0:r0 + 64, kb * 128:(kb + 1) * 128],
                                qs[r0:r0 + 64, nt * 512:(nt + 1) * 512],
                                start=True, stop=True,
                                deps=(qk_ready[0 + m], qk_ready[2 + m]))
                    slot_rd[0 + m].append(smm)
                    slot_rd[2 + m].append(smm)
                    exo = k.rd(sb, "scalar", lambda e, sb=sb, nt=nt, cs=colskip, hi=hi, m=m: e.activation(
                        out=PT[:, nt * 512 + cs:(nt + 1) * 512],
                        in_=k.pb[sb][:, cs:512],
                        func=AF.Exp, bias=bbias[:, m * 2 + (hi % 2):m * 2 + (hi % 2) + 1]),
                        deps=(bias_ops[hi],) + tuple(pt_rd_tile[nt]))
                    pt_rd_tile[nt] = []
                    bias_rd.append(exo)
                    dep_pv = exo
                    if nt == nt0:
                        dep_pv = tp.op("vector", lambda e, kb=kb: e.tensor_mul(
                            PT[:, kb * 128:(kb + 1) * 128], PT[:, kb * 128:(kb + 1) * 128],
                            consts["utri_bf"][:]), deps=(exo, cdeps["utri_bf"]))
                    vp = vpack[kb // 4]
                    vb0 = (kb % 4) * 260 + hi * 65
                    pv = k.mm1(nt, vp.ap()[:, vb0:vb0 + 65],
                               PT[:, nt * 512 + colskip:(nt + 1) * 512],
                               start=(kb == 0), stop=(kb == 4 * nt + 3),
                               deps=(dep_pv,) + tuple(vdone) + tuple(prev_norm.get(nt, ())),
                               out_ap=k.pb[nt][0:65, colskip:512])
                    pt_rd_tile[nt].append(pv)
                    pv_group[nt] = pv
            for nt in range(4):
                bank = nt
                def _rc(e, bank=bank):
                    with k.nc.allow_low_precision(reason="softmax 1/sum stored bf16 (matmul operand)"):
                        return e.reciprocal(rrow[0:1, 0:512], k.pb[bank][64:65, 0:512])
                rc = k.rd(bank, "vector", _rc, deps=tuple(prev_norm.get("rrow", ())))
                bmm = k.mm(5, [(consts["ones_row_bf"][0:1, 0:64], rrow[0:1, 0:512])], deps=(rc,),
                           out_ap=k.pb[5][0:64, 0:512])
                rbev = k.rd(5, "scalar", lambda e: e.copy(out=scr3[0:64, 0:512], in_=k.pb[5][0:64, 0:512]),
                            deps=tuple(scr3_rd))
                scr3_rd = []
                dsts = ACT2[m]
                evf = k.rd(bank, "vector", lambda e, bank=bank, dsts=dsts, r0=r0, nt=nt: e.tensor_mul(
                    dsts[r0:r0 + 64, nt * 512:(nt + 1) * 512], k.pb[bank][0:64, 0:512], scr3[0:64, 0:512]),
                    deps=(rbev,) + tuple(bufs.get("act2_war01", ()) if (hi == 0) else ()))
                scr3_rd.append(evf)
                prev_norm[nt] = (evf,)
                prev_norm["rrow"] = (bmm,)
                ot_evacs.append(evf)
        for nt in range(4):
            pt_rd.extend(pt_rd_tile[nt])

        for mi in range(NCC):
            for nt in range(4):
                sl = slice(nt * 512, (nt + 1) * 512)
                bank = 4 + (mi * 4 + nt) % 2
                calls = [(woT_block(m2, mi), ACT2[m2][:, sl]) for m2 in range(2)]
                gmm = k.mm(bank, calls, deps=tuple(woops) + tuple(ot_evacs))
                wt_rd.append(gmm)
                ad = k.rd(bank, "vector", lambda e, mi=mi, sl=sl, bank=bank: e.tensor_add(
                    X[mi][:, sl], X[mi][:, sl], k.pb[bank][:, 0:512]), deps=(x_new[mi],))
                x_new[mi] = ad
        bufs["act2_war01"] = tuple(wt_rd[-8:])
        arena_war = []

    bufs["arena_war"] = tuple(v for sl_ in slot_rd.values() for v in sl_[-4:]) + tuple(vdone[-2:])
    bufs["act2_war"] = tuple(wt_rd[-8:]) + tuple(scr3_rd)
    bufs["gb_war"] = tuple(pt_rd)
    bufs["act1_war"] = tuple(wt_rd[-16:])
    tp.ambient = ()
    return x_new
